# version 10
# baseline (speedup 1.0000x reference)
import numpy as np
import ml_dtypes
import concourse.bass as bass
import concourse.mybir as mybir
from concourse.bass_utils import run_bass_kernel_spmd

F32 = mybir.dt.float32
BF16 = mybir.dt.bfloat16
I32 = mybir.dt.int32
AF = mybir.ActivationFunctionType
ALU = mybir.AluOpType

SEQ, DIM, HID, KLEN = 8192, 1024, 3072, 4096
NFFT, P = 16384, 128
SCALE = float(np.float32(1.0 / 127.0))
EPS = 1e-5
SB = 512
NBLK = SEQ // SB
NCH = DIM // P
NHT = HID // P
NGRP = 8
MAGIC = 0x5F3759DF

ENGS = ("tensor", "vector", "scalar", "gpsimd", "sync")


def _host_constants():
    n = np.arange(P)
    W1 = np.exp(-2j * np.pi * np.outer(np.arange(64), np.arange(65)) / P)
    c_f1 = np.zeros((P, 130), np.float32)
    for h in (0, 1):
        c_f1[64 * h:64 * h + 64, 0:65] = W1.real
        c_f1[64 * h:64 * h + 64, 65:130] = W1.imag
    n1p = 2 * (n & 63) + (n >> 6)
    k1v = np.arange(64)
    G = np.zeros((P, 3, P, 64), np.float32)
    for k2 in range(P):
        g = (np.exp(-2j * np.pi * n1p * k2 / NFFT)[:, None]
             * np.exp(-2j * np.pi * np.outer(n1p, k1v) / P))
        G[:, 0, k2], G[:, 1, k2], G[:, 2, k2] = g.real, g.imag, -g.imag
    c_winv = np.zeros((P, 512), np.float32)
    Wi = np.exp(2j * np.pi * np.outer(k1v, n) / P)
    for h, W in ((0, Wi), (1, np.conj(Wi))):
        r = 64 * h
        c_winv[r:r + 64, 0:128] = W.real
        c_winv[r:r + 64, 128:256] = W.imag
        c_winv[r:r + 64, 256:384] = -W.imag
        c_winv[r:r + 64, 384:512] = W.real
    n2v = np.arange(64)
    Hc = np.zeros((P, P, 2, 64), np.float32)
    for k2 in range(P):
        h = ((2.0 / NFFT) * np.exp(2j * np.pi * n * k2 / NFFT)[:, None]
             * np.exp(2j * np.pi * k2 * n2v / P)[None, :])
        if k2 >= 64:
            h = np.conj(h)
        Hc[k2, :, 0], Hc[k2, :, 1] = h.real, -h.imag
    c_misc = np.zeros((P, 4), np.float32)
    c_misc[:, 0] = 1.0
    c_misc[:, 1] = np.where(n < 64, 1.0, -1.0)
    return {"c_f1": c_f1, "c_g": np.ascontiguousarray(G.reshape(P, -1)),
            "c_winv": c_winv, "c_h": np.ascontiguousarray(Hc.reshape(P, -1)),
            "c_misc": c_misc, "c_id": np.eye(P, dtype=np.float32)}


_CHAIN_OPS = ("tensor_", "scalar_tensor", "reduce_", "reciprocal", "memset",
              "memzero", "activation", "square", "sqrt", "mul", "add", "copy",
              "select", "transpose", "bn_", "sign", "engine_nop", "pool")


class ChainedEngine:

    def __init__(self, eng, sem):
        self._e = eng
        self._sem = sem
        self._k = 0

    def __getattr__(self, name):
        attr = getattr(self._e, name)
        if callable(attr) and any(name.startswith(p) for p in _CHAIN_OPS):
            def wrapped(*a, **kw):
                if self._k > 0:
                    self._e.wait_ge(self._sem, self._k)
                ins = attr(*a, **kw)
                ins.then_inc(self._sem, 1)
                self._k += 1
                return ins
            return wrapped
        return attr


class _CountStub:

    class _Ins:
        def then_inc(self, *a, **k):
            return self

    def __init__(self):
        self.k = 0

    def __getattr__(self, name):
        if any(name.startswith(p) for p in _CHAIN_OPS):
            def f(*a, **kw):
                self.k += 1
                return _CountStub._Ins()
            return f
        return lambda *a, **kw: _CountStub._Ins()


CHAINED = ("vector", "scalar", "gpsimd")


class Rec:
    def __init__(self):
        self.ops = {e: [] for e in ENGS}
        self.cnt = {}
        self.chain = {e: 0 for e in CHAINED}
        self.owner = {}
        self.mile = {}

    def n(self, sem):
        return self.cnt.setdefault(sem, 0)

    def add(self, eng, fn, waits=(), incs=()):
        wl = []
        for w in waits:
            wl.append((w[0], w[1]) if isinstance(w, tuple) else (w, self.n(w)))
        il = []
        nops = 0
        if eng in CHAINED:
            st = _CountStub()
            fn(st)
            nops = st.k
            self.chain[eng] += nops
        for s, k in incs:
            if k:
                self.cnt[s] = self.n(s) + k
                if eng in CHAINED and nops > 0:
                    self.owner.setdefault(s, set()).add(eng)
                    self.mile[(s, self.cnt[s])] = (eng, self.chain[eng])
                else:
                    assert s not in self.owner, f"sem {s} mixes chained/raw producers"
                    il.append((s, k))
        self.ops[eng].append((wl, fn, il, nops))

    def emit(self, eng_obj, name, sems, chain_sems):
        raw = eng_obj._e if isinstance(eng_obj, ChainedEngine) else eng_obj
        for wl, fn, il, nops in self.ops[name]:
            for s, v in wl:
                if v <= 0:
                    continue
                if s in self.owner:
                    need = {}
                    for vv in range(v, 0, -1):
                        e, c = self.mile[(s, vv)]
                        if e not in need:
                            need[e] = c
                        if len(need) == len(self.owner[s]):
                            break
                    for e, c in need.items():
                        raw.wait_ge(chain_sems[e], c)
                else:
                    raw.wait_ge(sems[s], v)
            k0 = eng_obj._k if isinstance(eng_obj, ChainedEngine) else 0
            ins = fn(eng_obj)
            if isinstance(eng_obj, ChainedEngine):
                assert eng_obj._k - k0 == nops, (name, nops, eng_obj._k - k0)
            for s, k in il:
                assert ins is not None
                ins = ins.then_inc(sems[s], k)


class KB:
    def __init__(self, taps=(), inject_x1=False,
                 phases=("p0a", "p0b", "p1", "p2", "p3")):
        self.taps = set(taps)
        self.inject_x1 = inject_x1
        self.phases = phases
        nc = self.nc = bass.Bass("TRN2", target_bir_lowering=False, debug=False)
        self.rec = Rec()
        self.C = _host_constants()
        self.sb_off = 17408
        self.sb_mark = []
        self.uid = 0
        d = nc.dram_tensor
        self.xT = d("xT", [DIM, SEQ], F32, kind="ExternalInput").ap()
        self.t_h = d("t", [KLEN, DIM], F32, kind="ExternalInput").ap()
        self.w_h = {"g": d("wg", [HID, DIM], I32, kind="ExternalInput").ap(),
                    "u": d("wu", [HID, DIM], I32, kind="ExternalInput").ap(),
                    "d": d("wd", [DIM, HID], I32, kind="ExternalInput").ap()}
        self.c_in = {k: d(k, list(v.shape), F32, kind="ExternalInput").ap()
                     for k, v in self.C.items()}
        self.out_h = d("out", [DIM, SEQ], F32, kind="ExternalOutput").ap()
        self.s_wnat = d("s_wnat", [2 * HID + DIM, HID], BF16, kind="Internal").ap()
        self.s_wT = {"g": d("s_wgT", [P, NCH * HID], BF16, kind="Internal").ap(),
                     "u": d("s_wuT", [P, NCH * HID], BF16, kind="Internal").ap(),
                     "d": d("s_wdT", [P, NHT * DIM], BF16, kind="Internal").ap()}
        self.s_x1 = d("s_x1", [DIM, SEQ], BF16, kind="Internal").ap()
        self.s_x2 = d("s_x2", [DIM, SEQ], BF16, kind="Internal").ap()
        self.s_ln2 = d("s_ln2", [2, SEQ], F32, kind="Internal").ap()
        self.s_rb = {t: d("s_rb_" + t, [2, SB], BF16, kind="Internal").ap() for t in ("f1", "f3")}
        self.s_that = d("s_that", [P, NGRP * 2 * SEQ], BF16, kind="Internal").ap()
        if inject_x1:
            self.x1_in = d("x1_in", [DIM, SEQ], BF16, kind="ExternalInput").ap()
            self.ln2_in = d("ln2_in", [2, SEQ], F32, kind="ExternalInput").ap()
        self.dbg = {}
        for tp, shape in (("x1", [DIM, SEQ]), ("x2", [DIM, SEQ]),
                          ("b0", [P, 2 * 65 * P]), ("yh0", [P, 2 * 64 * P]),
                          ("th0", [P, 2 * SEQ]), ("cc0", [P, 64 * 2 * P]),
                          ("win0", [P, 64 * P]), ("ln2rows", [2, SEQ])):
            if tp in self.taps:
                self.dbg[tp] = d("d_" + tp, shape, F32, kind="ExternalOutput").ap()

    def alloc(self, name, shape, dtype, align=64):
        byt = int(np.prod(shape[1:])) * (2 if dtype == BF16 else 4)
        self.sb_off = (self.sb_off + align - 1) // align * align
        t = self.nc.alloc_sbuf_tensor_at(name, list(shape), dtype, offset=self.sb_off)
        self.sb_off += byt
        assert self.sb_off <= 229000, f"SBUF overflow at {name}: {self.sb_off}"
        return t.ap()

    def mark(self):
        self.sb_mark.append(self.sb_off)

    def release(self):
        self.sb_off = self.sb_mark.pop()

    def nm(self, base):
        self.uid += 1
        return f"{base}{self.uid}"

    def barrier_waits(self):
        r = self.rec
        sems = ("thout", "s2t", "s1t", "nymmt", "nyevt", "tbin0", "ccst",
                "wTdone", "wsumev", "x2out", "x2ex", "i2x", "ptwx", "nyevx",
                "o_out", "x1out", "l2out", "dbgout", "s1ex", "i1x", "i1ex",
                "winldt", "s1et", "ptwt", "r2ld", "x1ld", "thld0", "thld1")
        return [(s, r.n(s)) for s in sems if r.n(s) > 0]

    def build(self):
        import contextlib
        nc, rec = self.nc, self.rec
        self.stk = contextlib.ExitStack()
        self.pbank = [self.stk.enter_context(nc.psum_tensor(f"pb{i}", [P, 512], F32))
                      for i in range(7)]
        self.pbf = self.stk.enter_context(nc.psum_tensor("pbf", [P, 1024], BF16))

        self.misc = self.alloc("misc", [P, 4], BF16)
        self.ident = self.alloc("ident", [P, P], BF16)
        self.tnyq = self.alloc("tnyq", [1, DIM], BF16)
        self.gsum = self.alloc("gsum", [1, HID], BF16)
        self.usum = self.alloc("usum", [1, HID], BF16)
        self.ones_row = self.alloc("ones_row", [1, P], BF16)

        rec.add("gpsimd", lambda g: g.dma_start(out=self.misc[:, :], in_=self.c_in["c_misc"][:, :]),
                incs=[("cst", 16)])
        rec.add("gpsimd", lambda g: g.dma_start(out=self.ident[:, :], in_=self.c_in["c_id"][:, :]),
                incs=[("cst", 16)])
        rec.add("vector", lambda v: v.memset(self.ones_row[:, :], 1.0), incs=[("cst1", 1)])

        if "p0b" in self.phases:
            self.conv_consts_load()
            self.p0b()
            self.release()
        if "p0a" in self.phases:
            if "p0b" not in self.phases:
                for k in "gud":
                    self.p0a_casts(k)
            self.p0a()
        if "p1" in self.phases:
            self.ffn_phase("f1")
        if self.inject_x1:
            rec.add("gpsimd", lambda g: g.dma_start(out=self.s_x1[:, :], in_=self.x1_in[:, :]),
                    incs=[("x1out", 16)])
            rec.add("gpsimd", lambda g: g.dma_start(out=self.s_ln2[:, :], in_=self.ln2_in[:, :]),
                    incs=[("l2out", 16)])
        if "p2" in self.phases:
            self.conv_consts_load()
            self.p2()
            self.release()
        if "p3" in self.phases:
            self.ffn_phase("f3")
        fw = [(s, rec.n(s)) for s in ("o_out", "x1out", "x2out", "thout", "l2out", "dbgout")
              if rec.n(s) > 0]
        rec.add("sync", lambda sy: sy.nop(), waits=fw)
        return self

    def p0a_casts(self, k):
        r0, R, Cc = {"g": (0, HID, DIM), "u": (HID, HID, DIM), "d": (2 * HID, DIM, HID)}[k]
        self.rec.add("gpsimd", lambda g, r0=r0, R=R, Cc=Cc, k=k: g.dma_start(
            out=self.s_wnat[r0:r0 + R, 0:Cc], in_=self.w_h[k][:, :]), incs=[("wnat" + k, 16)])

    def p0a(self):
        rec = self.rec
        wtmp = self.nc.alloc_sbuf_tensor_at(
            "wtmp", [P, NCH * HID], BF16, offset=229376 - NCH * HID * 2 - 64).ap()
        offs = {"g": (0, HID, DIM), "u": (HID, HID, DIM), "d": (2 * HID, DIM, HID)}
        for wi, k in enumerate("gud"):
            r0, R, Cc = offs[k]
            nch = Cc // P
            for c in range(nch):
                rec.add("sync", lambda sy, r0=r0, R=R, c=c, k=k: sy.dma_start_transpose(
                    wtmp[:, c * R:(c + 1) * R], self.s_wnat[r0:r0 + R, c * P:(c + 1) * P]),
                    waits=[("wnat" + k, 16), ("wTdone", rec.n("wTdone"))],
                    incs=[("wTsb", 16)])
            rec.add("sync", lambda sy, k=k, nch=nch, R=R: sy.dma_start(
                out=self.s_wT[k][:, 0:nch * R], in_=wtmp[:, 0:nch * R]),
                waits=[("wTsb", rec.n("wTsb"))], incs=[("wThbm", 16)])
            if k in "gu":
                row = self.gsum if k == "g" else self.usum
                pst = self.pbank[0]
                wtsb_target = rec.n("wTsb")
                for hq in range(NHT // 4):
                    for q in range(4):
                        for c in range(NCH):
                            ht = hq * 4 + q
                            rec.add("tensor", lambda t, c=c, ht=ht, q=q: t.matmul(
                                pst[0:1, q * 128:(q + 1) * 128], self.misc[:, 0:1],
                                wtmp[:, c * HID + ht * 128:c * HID + (ht + 1) * 128],
                                start=(c == 0), stop=(c == NCH - 1)),
                                waits=([("wTsb", wtsb_target), ("cst", 32),
                                        ("wsumev", rec.n("wsumev"))] if (c == 0 and q == 0) else []),
                                incs=[("wsum", 1)] if (c == NCH - 1 and q == 3) else ())
                    rec.add("vector", lambda v, row=row, hq=hq: v.tensor_copy(
                        row[0:1, hq * 512:(hq + 1) * 512], pst[0:1, 0:512]),
                        waits=[("wsum", rec.n("wsum"))], incs=[("wsumev", 1)])
            rec.add("sync", lambda sy: sy.nop(),
                    waits=[("wThbm", rec.n("wThbm")), ("wsumev", rec.n("wsumev"))],
                    incs=[("wTdone", 1)])

    def conv_consts_load(self):
        rec = self.rec
        self.mark()
        self.f1_sb = self.alloc(self.nm("f1sb"), [P, 130], BF16)
        self.g_sb = self.alloc(self.nm("gsb"), [P, 3 * P * 64], BF16)
        self.winv_sb = self.alloc(self.nm("winvsb"), [P, 512], BF16)
        self.hh_sb = self.alloc(self.nm("hhsb"), [P, P * 2 * 64], BF16)
        cw = self.barrier_waits() + [("wTdone", rec.n("wTdone")),
                                     ("ccst", rec.n("ccst"))]
        for src, dst in (("c_f1", self.f1_sb), ("c_g", self.g_sb),
                         ("c_winv", self.winv_sb), ("c_h", self.hh_sb)):
            rec.add("gpsimd", lambda g, src=src, dst=dst: g.dma_start(
                out=dst[:, :], in_=self.c_in[src][:, :]), waits=cw, incs=[("ccst", 16)])
        self.ccst_target = rec.n("ccst")

    def conv_group(self, g, mode, bufs):
        rec = self.rec
        f1_sb, g_sb, winv_sb, hh_sb = self.f1_sb, self.g_sb, self.winv_sb, self.hh_sb
        sfx = mode
        win, B = bufs["win"], bufs["B"]
        yh_re, yh_im = bufs["yhre"], bufs["yhim"]
        s_s1, s_s1e, s_s2, s_ptw = ("s1" + sfx, "s1e" + sfx, "s2" + sfx, "ptw" + sfx)
        nzrows = 32 if mode == "t" else 64
        winv = win.rearrange("p (j c) -> p j c", j=64)

        if mode == "t":
            tin = self.t_h.rearrange("(n2 j l) c -> l n2 j c", n2=32, j=64, l=2)
            for l in (0, 1):
                rec.add("gpsimd", lambda gg, l=l, g=g: gg.dma_start(
                    out=winv[64 * l:64 * l + 32, :, :],
                    in_=tin[l, :, :, g * P:(g + 1) * P]),
                    waits=[("winfree" + sfx, rec.n("winfree" + sfx))],
                    incs=[("winld" + sfx, 16)])
            wvw = ("winld" + sfx, rec.n("winld" + sfx))
        else:
            x1sl, r2n, b2n = bufs["x1sl"], bufs["r2n"], bufs["b2n"]
            x1v = x1sl.rearrange("p (n2 j l) -> p l n2 j", n2=64, j=64, l=2)
            tbanks = [self.pbf, self.pbank[3].bitcast(BF16), self.pbank[4].bitcast(BF16),
                      self.pbank[5].bitcast(BF16)]
            for jj in range(64):
                tb = tbanks[jj % 4]
                slot = (jj // 4) % 8
                pt = tb[:, slot * 128:slot * 128 + 128]
                tw = [("x1ld", rec.n("x1ld")), ("ccst", self.ccst_target), ("cst", 32)]
                if rec.n("wev" + sfx) >= 7:
                    tw.append(("wev" + sfx, rec.n("wev" + sfx) - 6))
                for l in (0, 1):
                    rec.add("tensor", lambda t, jj=jj, pt=pt, l=l: t.transpose(
                        pt[64 * l:64 * l + 64, :], x1v[:, l, :, jj], self.ident[:, :],
                        tile_position=(0, 64 * l)),
                        waits=tw if l == 0 else [], incs=[("wtr" + sfx, 1)] if l == 1 else ())
                for l in (0, 1):
                    if (jj + l) % 2 == 0:
                        rec.add("scalar", lambda s, jj=jj, l=l, pt=pt: s.activation(
                            winv[64 * l:64 * l + 64, jj, :],
                            pt[64 * l:64 * l + 64, :], AF.Identity,
                            bias=b2n[64 * l:64 * l + 64, 2 * jj + l:2 * jj + l + 1],
                            scale=r2n[64 * l:64 * l + 64, 2 * jj + l:2 * jj + l + 1]),
                            waits=[("wtr" + sfx, rec.n("wtr" + sfx))],
                            incs=[("wev" + sfx, 1)])
                    else:
                        rec.add("vector", lambda v, jj=jj, l=l, pt=pt: v.tensor_scalar(
                            winv[64 * l:64 * l + 64, jj, :],
                            pt[64 * l:64 * l + 64, :],
                            r2n[64 * l:64 * l + 64, 2 * jj + l:2 * jj + l + 1],
                            b2n[64 * l:64 * l + 64, 2 * jj + l:2 * jj + l + 1],
                            ALU.mult, ALU.add),
                            waits=[("wtr" + sfx, rec.n("wtr" + sfx))],
                            incs=[("wev" + sfx, 1)])
            wvw = ("wev" + sfx, rec.n("wev" + sfx))

        Bv = B.rearrange("p (r k c) -> p r k c", r=2, k=65)
        s1rot = [self.pbank[0], self.pbank[1], self.pbank[2], self.pbank[6]]
        for ch2 in range(64):
            pbank = s1rot[ch2 % 4]
            for ci in (0, 1):
                ch = 2 * ch2 + ci
                for l in (0, 1):
                    w = []
                    if ci == 0 and l == 0:
                        w = [wvw, ("ccst", self.ccst_target)]
                        if rec.n(s_s1e) >= 4:
                            w.append((s_s1e, rec.n(s_s1e) - 3))
                    rec.add("tensor", lambda t, ch=ch, ci=ci, l=l, pbank=pbank: t.matmul(
                        pbank[64 * l:64 * l + 64, ci * 256:ci * 256 + 130],
                        winv[64 * l:64 * l + nzrows, :, ch],
                        f1_sb[64 * l:64 * l + nzrows, :],
                        start=True, stop=True, tile_position=(64 * l, 64 * l)),
                        waits=w, incs=[(s_s1, 1)] if (ci == 1 and l == 1) else ())
            def s1ev(e, ch2=ch2, pbank=pbank):
                pin = pbank.rearrange("p (ci rk) -> p ci rk", ci=2)[:, :, 0:130]
                pin = pin.rearrange("p ci (r k) -> p ci r k", r=2)
                ov = Bv[:, :, :, 2 * ch2:2 * ch2 + 2].transpose([0, 3, 1, 2])
                if hasattr(e, "tensor_copy"):
                    return e.tensor_copy(ov, pin)
                return e.copy(ov, pin)
            if ch2 % 2 == 0:
                rec.add("vector", s1ev, waits=[(s_s1, rec.n(s_s1))], incs=[(s_s1e, 1)])
            else:
                rec.add("scalar", lambda s, ch2=ch2, pbank=pbank: s.copy(
                    Bv[:, :, :, 2 * ch2:2 * ch2 + 2].transpose([0, 3, 1, 2]),
                    pbank.rearrange("p (ci rk) -> p ci rk", ci=2)[:, :, 0:130].rearrange(
                        "p ci (r k) -> p ci r k", r=2)),
                    waits=[(s_s1, rec.n(s_s1))], incs=[(s_s1e, 1)])
        s1e_total = rec.n(s_s1e)

        pn = self.pbank[2]
        rec.add("tensor", lambda t: t.matmul(
            pn[0:1, 0:128], self.misc[:, 1:2], Bv[:, 0, 0, :], start=True, stop=True),
            waits=[(s_s1e, s1e_total), ("nyev" + sfx, rec.n("nyev" + sfx)), ("cst", 32)],
            incs=[("nymm" + sfx, 1)])
        if mode == "t":
            rec.add("vector", lambda v, g=g: v.tensor_scalar(
                self.tnyq[0:1, g * P:(g + 1) * P], pn[0:1, 0:128], 0.5, None, ALU.mult),
                waits=[("nymm" + sfx, rec.n("nymm" + sfx))], incs=[("nyev" + sfx, 1)])
        else:
            ypn, ynn, rowb = bufs["ypos"], bufs["yneg"], bufs["nyrow"]
            def nymath(v, g=g):
                v.tensor_mul(bufs["nyrowf"][0:1, :], pn[0:1, 0:128],
                             self.tnyq[0:1, g * P:(g + 1) * P])
                return v.tensor_scalar(rowb[0:1, :], bufs["nyrowf"][0:1, :],
                                       1.0 / NFFT, None, ALU.mult)
            rec.add("vector", nymath, waits=[("nymm" + sfx, rec.n("nymm" + sfx))],
                    incs=[("nyrow" + sfx, 1)])
            pnc = self.pbf[:, 0:128]
            rec.add("tensor", lambda t: t.transpose(
                pnc[:, 0:1], rowb[0:1, :], self.ident[0:1, 0:1]),
                waits=[("nyrow" + sfx, rec.n("nyrow" + sfx)),
                       ("nyev" + sfx, rec.n("nyev" + sfx))],
                incs=[("nytr" + sfx, 1)])
            def nycols(v):
                v.tensor_copy(ypn[:, 0:1], pnc[:, 0:1])
                return v.tensor_scalar(ynn[:, 0:1], pnc[:, 0:1], -1.0, None, ALU.mult)
            rec.add("vector", nycols, waits=[("nytr" + sfx, rec.n("nytr" + sfx))],
                    incs=[("nyev" + sfx, 1)])

        gv = g_sb.rearrange("p (s3 k q) -> p s3 k q", s3=3, k=P)
        if mode == "x":
            thre, thim = bufs["thre"], bufs["thim"]
            thld_base2 = [rec.n("thld0"), rec.n("thld1")]
            ptw_base = rec.n(s_ptw)
            for ck in (0, 1):
                for half, buf in ((0, thre), (1, thim)):
                    rec.add("sync", lambda sy, g=g, ck=ck, half=half, buf=buf: sy.dma_start(
                        out=buf[:, (ck % 2) * 1024:(ck % 2) * 1024 + 1024],
                        in_=self.s_that[:, g * 2 * SEQ + half * SEQ + ck * 1024:
                                        g * 2 * SEQ + half * SEQ + (ck + 1) * 1024]),
                        waits=[("thout", 16 * 2 * NGRP)] if (ck == 0 and half == 0) else [],
                        incs=[("thld%d" % (ck % 2), 16)])
        pre_rot = [self.pbank[3], self.pbank[5], self.pbank[0]]
        pim_rot = [self.pbank[4], self.pbank[6], self.pbank[1]]
        for jb in range(16):
            pre = pre_rot[jb % 3]
            pim = pim_rot[jb % 3]
            if mode == "x" and jb % 2 == 0 and jb // 2 + 2 < 8:
                ck = jb // 2 + 2
                for half, buf in ((0, bufs["thre"]), (1, bufs["thim"])):
                    rec.add("sync", lambda sy, g=g, ck=ck, half=half, buf=buf: sy.dma_start(
                        out=buf[:, (ck % 2) * 1024:(ck % 2) * 1024 + 1024],
                        in_=self.s_that[:, g * 2 * SEQ + half * SEQ + ck * 1024:
                                        g * 2 * SEQ + half * SEQ + (ck + 1) * 1024]),
                        waits=[(s_ptw, ptw_base + 2 * (ck - 1))] if half == 0 else [],
                        incs=[("thld%d" % (ck % 2), 16)])
            for jq in range(4):
                j = jb * 4 + jq
                m = 64 - j
                w0 = [(s_s1e, s1e_total)]
                rot = "xev" + sfx if mode == "x" else s_ptw
                if rec.n(rot) >= 3:
                    w0.append((rot, rec.n(rot) - 2))
                mm = [
                    (0, 0, j, 0, j, False), (0, 2, j, 1, j, True),
                    (1, 0, j, 1, j, False), (1, 1, j, 0, j, True),
                    (0, 0, 64 + j, 0, m, False), (0, 1, 64 + j, 1, m, True),
                    (1, 0, 64 + j, 1, m, False), (1, 2, 64 + j, 0, m, True),
                ]
                for (dst, gs, k2, br, bc, sp) in mm:
                    hi = k2 >= 64
                    pd = pim if dst else pre
                    rec.add("tensor", lambda t, pd=pd, gs=gs, k2=k2, br=br, bc=bc, jq=jq, hi=hi, sp=sp: t.matmul(
                        pd[64 * hi:64 * hi + 64, jq * 128:(jq + 1) * 128],
                        gv[:, gs, k2, :], Bv[:, br, bc, :],
                        start=(gs == 0), stop=sp, tile_position=(0, 64 * hi)),
                        waits=w0 if (dst == 0 and gs == 0 and not hi) else [],
                        incs=[(s_s2, 1)] if (dst == 1 and hi) and sp else ())
            if mode == "t":
                thre, thim = bufs["thre"], bufs["thim"]
                hf = (jb % 2) * 512
                def tev(e, pre=pre, pim=pim, hf=hf, jb=jb):
                    if jb == 0 and "th0" in self.dbg:
                        pass
                    if hasattr(e, "tensor_copy"):
                        e.tensor_copy(thre[:, hf:hf + 512], pre[:, :])
                        return e.tensor_copy(thim[:, hf:hf + 512], pim[:, :])
                    e.copy(thre[:, hf:hf + 512], pre[:, :])
                    return e.copy(thim[:, hf:hf + 512], pim[:, :])
                tvw = [(s_s2, rec.n(s_s2)), ("thout", rec.n("thout")),
                       ("dbgout", rec.n("dbgout"))]
                rec.add("vector" if jb % 2 == 0 else "scalar", tev, waits=tvw,
                        incs=[(s_ptw, 1)])
                if jb == 0:
                    rec.add("vector", lambda v: v.tensor_scalar(
                        thre[0:1, 0:128], thre[0:1, 0:128], 0.5, None, ALU.mult),
                        waits=[(s_ptw, rec.n(s_ptw))], incs=[("tbin0", 1)])
                if jb % 2 == 1:
                    ow = [(s_ptw, rec.n(s_ptw))]
                    if jb == 1:
                        ow.append(("tbin0", rec.n("tbin0")))
                    rec.add("sync", lambda sy, g=g, jb=jb: sy.dma_start(
                        out=self.s_that[:, g * 2 * SEQ + (jb - 1) * 512:
                                        g * 2 * SEQ + (jb + 1) * 512],
                        in_=thre[:, :]), waits=ow, incs=[("thout", 16)])
                    rec.add("sync", lambda sy, g=g, jb=jb: sy.dma_start(
                        out=self.s_that[:, g * 2 * SEQ + SEQ + (jb - 1) * 512:
                                        g * 2 * SEQ + SEQ + (jb + 1) * 512],
                        in_=thim[:, :]), incs=[("thout", 16)])
            else:
                tmp1 = bufs["ptmp1"]
                xo = (jb % 2) * 512
                xre_sb = bufs["xre"][:, xo:xo + 512]
                xim_sb = bufs["xim"][:, xo:xo + 512]
                co = ((jb // 2) % 2) * 1024 + (jb % 2) * 512
                def xev(s, pre=pre, pim=pim, xre_sb=xre_sb, xim_sb=xim_sb):
                    s.copy(xre_sb, pre[:, :])
                    return s.copy(xim_sb, pim[:, :])
                xw = [(s_s2, rec.n(s_s2))]
                if rec.n(s_ptw) >= 2:
                    xw.append((s_ptw, rec.n(s_ptw) - 1))
                rec.add("scalar", xev, waits=xw, incs=[("xev" + sfx, 1)])
                ck = jb // 2
                def ptw(v, jb=jb, co=co, xre_sb=xre_sb, xim_sb=xim_sb):
                    tr = bufs["thre"][:, co:co + 512]
                    ti = bufs["thim"][:, co:co + 512]
                    yr = yh_re[:, jb * 512:(jb + 1) * 512]
                    yi = yh_im[:, jb * 512:(jb + 1) * 512]
                    v.tensor_mul(yr, xre_sb, tr)
                    v.tensor_mul(tmp1[:, :], xim_sb, ti)
                    v.tensor_sub(yr, yr, tmp1[:, :])
                    v.tensor_mul(yi, xre_sb, ti)
                    v.tensor_mul(tmp1[:, :], xim_sb, tr)
                    return v.tensor_add(yi, yi, tmp1[:, :])
                rec.add("vector", ptw,
                        waits=[("xev" + sfx, rec.n("xev" + sfx)),
                               ("thld%d" % (ck % 2), 32 * (ck // 2 + 1) + thld_base2[ck % 2])],
                        incs=[(s_ptw, 1)])

        if mode == "t":
            rec.add("sync", lambda sy: sy.nop(), waits=[(s_s1, rec.n(s_s1))],
                    incs=[("winfree" + sfx, 1)])
            return

        Cc, x2sl, x1sl = bufs["Cc"], bufs["x2sl"], bufs["x1sl"]
        ypn, ynn = bufs["ypos"], bufs["yneg"]
        ptw_total = rec.n(s_ptw)
        yre_v = yh_re.rearrange("p (k c) -> p k c", k=64)
        yim_v = yh_im.rearrange("p (k c) -> p k c", k=64)
        ccv = Cc.rearrange("p (c r n) -> p c r n", c=64, r=2)
        hv = hh_sb.rearrange("p (n s2 q) -> p n s2 q", n=P, s2=2)
        x1v2 = x1sl.rearrange("p (n2 n1) -> p n1 n2", n2=64)
        x2v = x2sl.rearrange("p (n1 n2) -> p n1 n2", n1=P)
        i1_rot = [self.pbank[0], self.pbank[1], self.pbank[3]]
        for hh in (0, 1):
            for cp in range(32):
                pbank = i1_rot[cp % 3]
                for ci in (0, 1):
                    ch = hh * 64 + 2 * cp + ci
                    for hi in (0, 1):
                        w = []
                        if ci == 0 and hi == 0:
                            w = [(s_ptw, ptw_total), ("i2" + sfx, rec.n("i2" + sfx))]
                            if rec.n("i1e" + sfx) >= 3:
                                w.append(("i1e" + sfx, rec.n("i1e" + sfx) - 2))
                        rec.add("tensor", lambda t, ch=ch, ci=ci, hi=hi, pbank=pbank: t.matmul(
                            pbank[64 * hi:64 * hi + 64, ci * 256:(ci + 1) * 256],
                            yre_v[64 * hi:64 * hi + 64, :, ch],
                            winv_sb[64 * hi:64 * hi + 64, 0:256],
                            start=True, stop=False, tile_position=(64 * hi, 64 * hi)),
                            waits=w)
                        rec.add("tensor", lambda t, ch=ch, ci=ci, hi=hi, pbank=pbank: t.matmul(
                            pbank[64 * hi:64 * hi + 64, ci * 256:(ci + 1) * 256],
                            yim_v[64 * hi:64 * hi + 64, :, ch],
                            winv_sb[64 * hi:64 * hi + 64, 256:512],
                            start=False, stop=True, tile_position=(64 * hi, 64 * hi)),
                            incs=[("i1" + sfx, 1)] if (ci == 1 and hi == 1) else ())
                def i1ev(e, cp=cp, pbank=pbank):
                    pin = pbank.rearrange("p (ci rn) -> p ci rn", ci=2)
                    ov = ccv[:, 2 * cp:2 * cp + 2, :, :].rearrange("p c r n -> p c (r n)")
                    if hasattr(e, "tensor_copy"):
                        return e.tensor_copy(ov, pin)
                    return e.copy(ov, pin)
                rec.add("vector" if cp % 2 == 0 else "scalar", i1ev,
                        waits=[("i1" + sfx, rec.n("i1" + sfx))],
                        incs=[("i1e" + sfx, 1)])
            i1e_half = rec.n("i1e" + sfx)
            i2_rot = [self.pbank[2], self.pbank[4], self.pbank[5]]
            for nb in range(16):
                pb = i2_rot[nb % 3]
                for nq in range(8):
                    n1 = nb * 8 + nq
                    w = []
                    if nq == 0:
                        w = [("i1e" + sfx, i1e_half)]
                        if rec.n("x2e" + sfx) >= 3:
                            w.append(("x2e" + sfx, rec.n("x2e" + sfx) - 2))
                    rec.add("tensor", lambda t, n1=n1, nq=nq, hh=hh, pb=pb: t.matmul(
                        pb[64 * hh:64 * hh + 64, nq * 64:(nq + 1) * 64],
                        ccv[:, :, 0, n1], hv[:, n1, 0, :],
                        start=True, stop=False, tile_position=(0, 64 * hh)),
                        waits=w)
                    rec.add("tensor", lambda t, n1=n1, nq=nq, hh=hh, pb=pb: t.matmul(
                        pb[64 * hh:64 * hh + 64, nq * 64:(nq + 1) * 64],
                        ccv[:, :, 1, n1], hv[:, n1, 1, :],
                        start=False, stop=True, tile_position=(0, 64 * hh)),
                        incs=[("i2" + sfx, 1)] if nq == 7 else ())
                pbv = pb.rearrange("p (nq n2) -> p nq n2", nq=8)
                def x2ev(v, nb=nb, hh=hh, pbv=pbv):
                    r0 = 64 * hh
                    for par in (0, 1):
                        ny = ypn if par == 0 else ynn
                        v.scalar_tensor_tensor(
                            x2v[r0:r0 + 64, nb * 8 + par:nb * 8 + 8:2, :],
                            pbv[r0:r0 + 64, par:8:2, :], ny[r0:r0 + 64, 0:1],
                            x1v2[r0:r0 + 64, nb * 8 + par:nb * 8 + 8:2, :],
                            ALU.add, ALU.add)
                    return v.engine_nop()
                rec.add("vector", x2ev,
                        waits=[("i2" + sfx, rec.n("i2" + sfx)),
                               ("nyev" + sfx, rec.n("nyev" + sfx))],
                        incs=[("x2e" + sfx, 1)])
        rec.add("sync", lambda sy, g=g: sy.dma_start(
            out=self.s_x2[g * P:(g + 1) * P, :], in_=x2sl[:, :]),
            waits=[("x2e" + sfx, rec.n("x2e" + sfx))], incs=[("x2out", 16)])
        if "x2" in self.dbg:
            rec.add("gpsimd", lambda gg, g=g: gg.dma_start(
                out=self.dbg["x2"][g * P:(g + 1) * P, :], in_=x2sl[:, :]),
                waits=[("x2e" + sfx, rec.n("x2e" + sfx))], incs=[("dbgout", 16)])
        if "b0" in self.dbg and g == 0:
            self.dbg_dump("b0", [(B, 0, 2 * 65 * P)], [(s_s1e, s1e_total)])
        if "yh0" in self.dbg and g == 0:
            self.dbg_dump("yh0", [(yh_re, 0, 64 * P), (yh_im, 64 * P, 64 * P)],
                          [(s_ptw, ptw_total)])
        if "win0" in self.dbg and g == 0:
            self.dbg_dump("win0", [(win, 0, 64 * P)], [("wev" + sfx, rec.n("wev" + sfx))])
        if "cc0" in self.dbg and g == 0:
            self.dbg_dump("cc0", [(Cc, 0, 64 * 2 * P)], [("i1e" + sfx, rec.n("i1e" + sfx))])

    def dbg_dump(self, key, pieces, waits):
        for (buf, off, ln) in pieces:
            self.rec.add("gpsimd", lambda gg, buf=buf, off=off, ln=ln, key=key: gg.dma_start(
                out=self.dbg[key][:, off:off + ln], in_=buf[:, 0:ln]),
                waits=waits, incs=[("dbgout", 16)])

    def p0b(self):
        self.mark()
        bufs = {"win": self.alloc("t_win", [P, 64 * P], BF16),
                "B": self.alloc("t_B", [P, 2 * 65 * P], BF16),
                "yhre": None, "yhim": None,
                "thre": self.alloc("t_thre", [P, 1024], BF16),
                "thim": self.alloc("t_thim", [P, 1024], BF16)}
        for g in range(NGRP):
            if "p0a" in self.phases and g in (1, 2, 3):
                self.p0a_casts("gud"[g - 1])
            self.conv_group(g, "t", bufs)
        self.release()

    def p2(self):
        rec = self.rec
        self.mark()
        bufs = {
            "x1sl": self.alloc("x1sl", [P, SEQ], BF16),
            "win": self.alloc("p2win", [P, 64 * P], BF16),
            "B": self.alloc("p2B", [P, 2 * 65 * P], BF16),
            "yhre": self.alloc("yhre", [P, 64 * P], BF16),
            "yhim": self.alloc("yhim", [P, 64 * P], BF16),
            "thre": self.alloc("thre", [P, 2048], BF16),
            "thim": self.alloc("thim", [P, 2048], BF16),

            "r2n": self.alloc("r2n", [P, P], F32),
            "b2n": self.alloc("b2n", [P, P], F32),
            "ypos": self.alloc("ypos", [P, 1], F32),
            "yneg": self.alloc("yneg", [P, 1], F32),
            "ptmp1": self.alloc("ptmp1", [P, 512], BF16),
            "xre": self.alloc("xre", [P, 1024], BF16),
            "xim": self.alloc("xim", [P, 1024], BF16),
        }
        bufs["nyrow"] = bufs["ptmp1"][0:1, 0:P]
        bufs["nyrowf"] = bufs["xre"].bitcast(F32)[0:1, 0:P]
        bufs["Cc"] = bufs["B"][:, 0:64 * 2 * P]
        bufs["x2sl"] = bufs["win"][:, 0:SEQ]
        bufs["gtmp"] = bufs["win"][:, 4096:5120]
        l2w = [("l2out", rec.n("l2out"))] + self.barrier_waits()
        for l in (0, 1):
            rec.add("gpsimd", lambda gg, l=l: gg.dma_start(
                out=bufs["r2n"][64 * l:64 * l + 64, :],
                in_=self.s_ln2[0:1, :].rearrange("o (n2 n1) -> (o n2) n1", n2=64)),
                waits=l2w, incs=[("r2ld", 16)])
            rec.add("gpsimd", lambda gg, l=l: gg.dma_start(
                out=bufs["b2n"][64 * l:64 * l + 64, :],
                in_=self.s_ln2[1:2, :].rearrange("o (n2 n1) -> (o n2) n1", n2=64)),
                waits=l2w, incs=[("r2ld", 16)])
        pb2w = self.barrier_waits()
        for g in range(NGRP):
            rec.add("sync", lambda sy, g=g: sy.dma_start(
                out=bufs["x1sl"][:, :], in_=self.s_x1[g * P:(g + 1) * P, :]),
                waits=[("x1out", rec.n("x1out")), ("r2ld", rec.n("r2ld")),
                       ("x2ex", rec.n("x2ex")), ("x2out", rec.n("x2out")),
                       ("thout", rec.n("thout")), ("dbgout", rec.n("dbgout"))]
                      + (pb2w if g == 0 else []),
                incs=[("x1ld", 16)])
            self.conv_group(g, "x", bufs)
        self.release()

    def ffn_phase(self, tag):
        rec = self.rec
        self.mark()
        wgT = self.alloc(tag + "wgT", [P, NCH * HID], BF16)
        wuT = self.alloc(tag + "wuT", [P, NCH * HID], BF16)
        nxs = 2
        xsl = [self.alloc(tag + f"xsl{i}", [P, NCH * SB], BF16) for i in range(nxs)]
        lnsl = [self.alloc(tag + f"lnsl{i}", [P, NCH * SB], BF16) for i in (0, 1)]
        hid = self.alloc(tag + "hid", [P, NHT * SB], BF16)
        silu = self.alloc(tag + "silu", [P, 2 * SB], BF16)
        rbp = self.alloc(tag + "rbp", [P, SB], BF16)
        rows = self.alloc(tag + "rows", [1, 6 * SB], F32)
        rows_bf = self.alloc(tag + "rowsbf", [1, 2 * SB], BF16)
        res = self.alloc(tag + "res", [P, NCH * SB], BF16 if tag == "f1" else F32)
        T = lambda s: tag + s
        f1 = tag == "f1"
        wdT = wgT[:, 0:NHT * DIM]

        bw = self.barrier_waits()
        for k, dst in (("g", wgT), ("u", wuT)):
            rec.add("sync", lambda sy, k=k, dst=dst: sy.dma_start(
                out=dst[:, :], in_=self.s_wT[k][:, :]),
                waits=[("wThbm", 48)] + bw, incs=[(T("wld" + k), 16)])

        pstat = self.pbank[0]
        pstat2 = self.pbank[1]
        pg = [self.pbank[2], self.pbank[3]]
        pu = [self.pbank[4], self.pbank[5]]
        pdn = [self.pbank[6], self.pbf.bitcast(F32)]
        xin = (self.xT if f1 else self.s_x2).rearrange("(c p) s -> p c s", p=P)

        def prologue(i):
            bi = i % 2
            xs = i % nxs
            lw = [("wsumev", 12 if "p0a" in self.phases else 0)]
            if i == 0:
                lw += bw
            if i >= nxs:
                lw.append((T("x1ev"), 8 * (i - nxs + 1)))
            if not f1:
                lw.append(("x2out", 16 * NGRP if "p2" in self.phases else 0))
            rec.add("gpsimd", lambda gg, i=i, xs=xs: gg.dma_start(
                out=xsl[xs].rearrange("p (c s) -> p c s", c=NCH),
                in_=xin[:, :, i * SB:(i + 1) * SB]),
                waits=[w for w in lw if w[1] > 0], incs=[(T("xld%d" % xs), 16)])
            sqw = [(T("xld%d" % xs), 16 * (i // nxs + 1)), (T("std"), rec.n(T("std")))]
            if i >= 2:
                sqw.append((T("umm"), rec.n(T("umm")) - NHT))
                if f1:
                    sqw.append((T("l2mm"), rec.n(T("l2mm"))))
            rec.add("scalar", lambda s, bi=bi, xs=xs: s.activation(
                lnsl[bi][:, :], xsl[xs][:, :], AF.Square),
                waits=sqw, incs=[(T("sqd"), 1)])
            stw = [(T("xld%d" % xs), 16 * (i // nxs + 1)), (T("row"), rec.n(T("row"))),
                   ("cst", 32), (T("l2row"), rec.n(T("l2row")))]
            if "p0a" in self.phases:
                stw.append(("wsumev", 12))
            for c in range(NCH):
                rec.add("tensor", lambda t, xs=xs, c=c: t.matmul(
                    pstat[0:1, 0:SB], self.misc[:, 0:1], xsl[xs][:, c * SB:(c + 1) * SB],
                    start=(c == 0), stop=(c == NCH - 1)),
                    waits=stw if c == 0 else [])
            for c in range(NCH):
                rec.add("tensor", lambda t, bi=bi, c=c: t.matmul(
                    pstat2[0:1, 0:SB], self.misc[:, 0:1], lnsl[bi][:, c * SB:(c + 1) * SB],
                    start=(c == 0), stop=(c == NCH - 1)),
                    waits=[(T("sqd"), rec.n(T("sqd")))] if c == 0 else [],
                    incs=[(T("std"), 1)] if c == NCH - 1 else ())
            def rowmath(v):
                s_, vv, y, tmp = (rows[0:1, k * SB:(k + 1) * SB] for k in range(4))
                v.tensor_copy(s_, pstat[0:1, 0:SB])
                v.tensor_mul(tmp, s_, s_)
                v.scalar_tensor_tensor(vv, tmp, -1.0 / DIM, pstat2[0:1, 0:SB],
                                       ALU.mult, ALU.add)
                v.tensor_scalar(vv, vv, 1.0, DIM * EPS, ALU.mult, ALU.add)
                v.tensor_single_scalar(y.bitcast(I32), vv.bitcast(I32), 1, ALU.logical_shift_right)
                v.tensor_scalar(y.bitcast(I32), y.bitcast(I32), -1, MAGIC, ALU.mult, ALU.add)
                for _ in range(4):
                    v.tensor_mul(tmp, y, y)
                    v.tensor_mul(tmp, tmp, vv)
                    v.tensor_scalar(tmp, tmp, -0.5, 1.5, ALU.mult, ALU.add)
                    v.tensor_mul(y, y, tmp)
                v.tensor_scalar(rows_bf[0:1, 0:SB], y, float(np.sqrt(DIM)), None, ALU.mult)
                return v.scalar_tensor_tensor(rows_bf[0:1, SB:2 * SB], s_,
                                              -1.0 / float(np.sqrt(DIM)), y,
                                              ALU.mult, ALU.mult)
            rmw = [(T("std"), rec.n(T("std")))]
            if f1:
                rmw.append(("l2out", rec.n("l2out")))
                if "ln2rows" in self.dbg:
                    rmw.append(("dbgout", rec.n("dbgout")))
            rec.add("vector", rowmath, waits=rmw, incs=[(T("row"), 1)])
            rec.add("sync", lambda sy, i=i: sy.dma_start(
                out=self.s_rb[tag][i % 2:i % 2 + 1, :], in_=rows_bf[0:1, 0:SB]),
                waits=[(T("row"), rec.n(T("row"))), (T("rbro"), rec.n(T("rbro")) - 16)
                       if rec.n(T("rbro")) >= 32 else (T("row"), rec.n(T("row")))],
                incs=[(T("rbro"), 16)])
        def prologue_b(i):
            bi = i % 2
            xs = i % nxs
            rec.add("gpsimd", lambda gg, i=i: gg.dma_start(
                out=rbp[:, :], in_=self.s_rb[tag][i % 2:i % 2 + 1, :].broadcast_to([P, SB])),
                waits=[(T("rbro"), 16 * (i + 1)), (T("lnap"), rec.n(T("lnap")))],
                incs=[(T("rbld"), 16)])
            for c in range(NCH):
                rec.add("vector", lambda v, bi=bi, xs=xs, c=c: v.tensor_mul(
                    lnsl[bi][:, c * SB:(c + 1) * SB], xsl[xs][:, c * SB:(c + 1) * SB], rbp[:, :]),
                    waits=[(T("rbld"), rec.n(T("rbld"))), (T("std"), rec.n(T("std")))] if c == 0 else [],
                    incs=[(T("lnap"), 1)])

        def gate_up(i):
            bi = i % 2
            for ht in range(NHT):
                q = ht % 2
                gw = [(T("lnap"), rec.n(T("lnap"))), (T("wldg"), 16 * (i + 1))]
                if rec.n(T("silu")) >= 2:
                    gw.append((T("silu"), rec.n(T("silu")) - 1))
                for c in range(NCH):
                    if c == NCH - 1:
                        rec.add("tensor", lambda t, ht=ht, q=q: t.matmul(
                            pg[q][:, 0:SB], self.gsum[0:1, ht * P:(ht + 1) * P],
                            rows_bf[0:1, SB:2 * SB], start=False, stop=False,
                            skip_group_check=True))
                    rec.add("tensor", lambda t, bi=bi, c=c, ht=ht, q=q: t.matmul(
                        pg[q][:, 0:SB], wgT[:, c * HID + ht * P:c * HID + (ht + 1) * P],
                        lnsl[bi][:, c * SB:(c + 1) * SB], start=(c == 0),
                        stop=(c == NCH - 1)),
                        waits=gw if c == 0 else [],
                        incs=[(T("gmm"), 1)] if c == NCH - 1 else ())
                uw = [(T("wldu"), 16)]
                if rec.n(T("hmul")) >= 2:
                    uw.append((T("hmul"), rec.n(T("hmul")) - 1))
                for c in range(NCH):
                    if c == NCH - 1:
                        rec.add("tensor", lambda t, ht=ht, q=q: t.matmul(
                            pu[q][:, 0:SB], self.usum[0:1, ht * P:(ht + 1) * P],
                            rows_bf[0:1, SB:2 * SB], start=False, stop=False,
                            skip_group_check=True))
                    rec.add("tensor", lambda t, bi=bi, c=c, ht=ht, q=q: t.matmul(
                        pu[q][:, 0:SB], wuT[:, c * HID + ht * P:c * HID + (ht + 1) * P],
                        lnsl[bi][:, c * SB:(c + 1) * SB], start=(c == 0),
                        stop=(c == NCH - 1)),
                        waits=uw if c == 0 else [],
                        incs=[(T("umm"), 1)] if c == NCH - 1 else ())
                sw = [(T("gmm"), rec.n(T("gmm")))]
                if rec.n(T("hmul")) >= 1:
                    sw.append((T("hmul"), rec.n(T("hmul")) - 1))
                rec.add("scalar", lambda s, ht=ht, q=q: s.activation(
                    silu[:, q * SB:(q + 1) * SB], pg[q][:, 0:SB], AF.Silu, scale=SCALE),
                    waits=sw, incs=[(T("silu"), 1)])
                hw = [(T("silu"), rec.n(T("silu"))), (T("umm"), rec.n(T("umm")))]
                if i > 0 and ht == 0:
                    hw.append((T("dmm"), rec.n(T("dmm"))))
                rec.add("vector", lambda v, ht=ht, q=q: v.tensor_mul(
                    hid[:, ht * SB:(ht + 1) * SB], silu[:, q * SB:(q + 1) * SB], pu[q][:, 0:SB]),
                    waits=hw, incs=[(T("hmul"), 1)])
                if ht == NHT - 1:
                    rec.add("gpsimd", lambda gg: gg.dma_start(
                        out=wdT[:, :], in_=self.s_wT["d"][:, :]),
                        waits=[(T("gmm"), rec.n(T("gmm"))), ("wThbm", 48)],
                        incs=[(T("wldd"), 16)])

        def down_store(i):
            bi = i % 2
            xs = i % nxs
            hm = rec.n(T("hmul"))
            for dt in range(NCH):
                pd = pdn[dt % 2]
                dw = []
                if dt == 0:
                    dw = [(T("hmul"), hm), (T("wldd"), 16 * (i + 1))]
                    if i > 0:
                        dw.append((("x1out" if f1 else "o_out"), 16 * i))
                    if f1:
                        dw.append((T("l2row"), rec.n(T("l2row"))))
                    if "x1" in self.dbg and f1:
                        dw.append(("dbgout", rec.n("dbgout")))
                if dt >= 2:
                    dw.append((T("sdev"), rec.n(T("sdev")) - 1))
                for hc in range(NHT):
                    rec.add("tensor", lambda t, dt=dt, hc=hc, pd=pd: t.matmul(
                        pd[:, 0:SB] if True else None,
                        wdT[:, hc * DIM + dt * P:hc * DIM + (dt + 1) * P],
                        hid[:, hc * SB:(hc + 1) * SB], start=(hc == 0), stop=(hc == NHT - 1)),
                        waits=dw if hc == 0 else [],
                        incs=[(T("dmm"), 1)] if hc == NHT - 1 else ())
                rec.add("scalar", lambda s, dt=dt, pd=pd: s.activation(
                    res[:, dt * SB:(dt + 1) * SB], pd[:, 0:SB], AF.Copy,
                    scale=0.5 * SCALE * SCALE),
                    waits=[(T("dmm"), rec.n(T("dmm")))],
                    incs=[(T("sdev"), 1)])
                rec.add("vector", lambda v, xs=xs, dt=dt: v.tensor_add(
                    res[:, dt * SB:(dt + 1) * SB], res[:, dt * SB:(dt + 1) * SB],
                    xsl[xs][:, dt * SB:(dt + 1) * SB]),
                    waits=[(T("sdev"), rec.n(T("sdev")))],
                    incs=[(T("x1ev"), 1)])
            if i + 1 < NBLK:
                rec.add("sync", lambda sy: sy.dma_start(
                    out=wgT[:, 0:NCH * HID], in_=self.s_wT["g"][:, :]),
                    waits=[(T("dmm"), rec.n(T("dmm")))], incs=[(T("wldg"), 16)])
            if f1:
                x1o = self.s_x1.rearrange("(c p) s -> p c s", p=P)
                rec.add("sync", lambda sy, i=i: sy.dma_start(
                    out=x1o[:, :, i * SB:(i + 1) * SB],
                    in_=res.rearrange("p (c s) -> p c s", c=NCH)),
                    waits=[(T("x1ev"), rec.n(T("x1ev"))), ("x1out", 16 * i)],
                    incs=[("x1out", 16)])
                if "x1" in self.dbg:
                    d1 = self.dbg["x1"].rearrange("(c p) s -> p c s", p=P)
                    rec.add("gpsimd", lambda gg, i=i: gg.dma_start(
                        out=d1[:, :, i * SB:(i + 1) * SB],
                        in_=res.rearrange("p (c s) -> p c s", c=NCH)),
                        waits=[(T("x1ev"), rec.n(T("x1ev")))], incs=[("dbgout", 16)])
            else:
                oo = self.out_h.rearrange("(c p) s -> p c s", p=P)
                rec.add("sync", lambda sy, i=i: sy.dma_start(
                    out=oo[:, :, i * SB:(i + 1) * SB],
                    in_=res.rearrange("p (c s) -> p c s", c=NCH)),
                    waits=[(T("x1ev"), rec.n(T("x1ev"))), ("o_out", 16 * i)],
                    incs=[("o_out", 16)])
            if f1:
                rec.add("scalar", lambda s, bi=bi: s.activation(
                    lnsl[bi][:, :], res[:, :], AF.Square),
                    waits=[(T("x1ev"), rec.n(T("x1ev"))), (T("l2mm"), rec.n(T("l2mm"))),
                           (T("umm"), rec.n(T("umm")))],
                    incs=[(T("l2sq"), 1)])
                l2w0 = [(T("x1ev"), rec.n(T("x1ev"))), (T("l2row"), rec.n(T("l2row"))),
                        (T("row"), rec.n(T("row")))]
                for c in range(NCH):
                    rec.add("tensor", lambda t, c=c: t.matmul(
                        pstat[0:1, 0:SB], self.misc[:, 0:1],
                        res[:, c * SB:(c + 1) * SB], start=(c == 0), stop=(c == NCH - 1)),
                        waits=l2w0 if c == 0 else [])
                for c in range(NCH):
                    rec.add("tensor", lambda t, bi=bi, c=c: t.matmul(
                        pstat2[0:1, 0:SB], self.misc[:, 0:1],
                        lnsl[bi][:, c * SB:(c + 1) * SB], start=(c == 0), stop=(c == NCH - 1)),
                        waits=[(T("l2sq"), rec.n(T("l2sq")))] if c == 0 else [],
                        incs=[(T("l2mm"), 1)] if c == NCH - 1 else ())
                def row2(v):
                    s_, vv, y, tmp = (rows[0:1, k * SB:(k + 1) * SB] for k in range(4))
                    r2o = rows[0:1, 2 * SB:3 * SB]
                    b2 = rows[0:1, 5 * SB:6 * SB]
                    v.tensor_copy(s_, pstat[0:1, 0:SB])
                    v.tensor_mul(tmp, s_, s_)
                    v.scalar_tensor_tensor(vv, tmp, -1.0 / DIM, pstat2[0:1, 0:SB],
                                           ALU.mult, ALU.add)
                    v.tensor_scalar(vv, vv, 1.0, DIM * EPS, ALU.mult, ALU.add)
                    v.tensor_single_scalar(y.bitcast(I32), vv.bitcast(I32), 1, ALU.logical_shift_right)
                    v.tensor_scalar(y.bitcast(I32), y.bitcast(I32), -1, MAGIC, ALU.mult, ALU.add)
                    for _ in range(4):
                        v.tensor_mul(tmp, y, y)
                        v.tensor_mul(tmp, tmp, vv)
                        v.tensor_scalar(tmp, tmp, -0.5, 1.5, ALU.mult, ALU.add)
                        v.tensor_mul(y, y, tmp)
                    v.scalar_tensor_tensor(b2, s_, -1.0 / float(np.sqrt(DIM)), y,
                                           ALU.mult, ALU.mult)
                    return v.tensor_scalar(r2o, y, float(np.sqrt(DIM)), None, ALU.mult)
                rec.add("vector", row2, waits=[(T("l2mm"), rec.n(T("l2mm"))),
                                               ("l2out", rec.n("l2out"))],
                        incs=[(T("l2row"), 1)])
                rec.add("sync", lambda sy, i=i: sy.dma_start(
                    out=self.s_ln2[0:1, i * SB:(i + 1) * SB], in_=rows[0:1, 2 * SB:3 * SB]),
                    waits=[(T("l2row"), rec.n(T("l2row"))), ("l2out", rec.n("l2out"))],
                    incs=[("l2out", 16)])
                rec.add("sync", lambda sy, i=i: sy.dma_start(
                    out=self.s_ln2[1:2, i * SB:(i + 1) * SB], in_=rows[0:1, 5 * SB:6 * SB]),
                    incs=[("l2out", 16)])
                if "ln2rows" in self.dbg:
                    rec.add("gpsimd", lambda gg, i=i: gg.dma_start(
                        out=self.dbg["ln2rows"][0:1, i * SB:(i + 1) * SB],
                        in_=rows[0:1, 2 * SB:3 * SB]),
                        waits=[(T("l2row"), rec.n(T("l2row")))], incs=[("dbgout", 16)])
                    rec.add("gpsimd", lambda gg, i=i: gg.dma_start(
                        out=self.dbg["ln2rows"][1:2, i * SB:(i + 1) * SB],
                        in_=rows[0:1, 5 * SB:6 * SB]),
                        incs=[("dbgout", 16)])

        prologue(0)
        prologue_b(0)
        for i in range(NBLK):
            gate_up(i)
            if i + 1 < NBLK:
                prologue(i + 1)
                prologue_b(i + 1)
            down_store(i)
        self.release()

    def finalize(self):
        import contextlib
        nc, rec = self.nc, self.rec
        wnames = set()
        for e in ENGS:
            for wl, _, _, _ in rec.ops[e]:
                for s, v in wl:
                    wnames.add(s)
        names = sorted(set(rec.cnt.keys()) | wnames)
        assert len(names) <= 200, len(names)
        with contextlib.ExitStack() as sx:
            names = [s for s in names if s not in rec.owner]
            sems = {s: sx.enter_context(nc.semaphore(s)) for s in names}
            chv = sx.enter_context(nc.semaphore("chain_v"))
            chs = sx.enter_context(nc.semaphore("chain_s"))
            chg = sx.enter_context(nc.semaphore("chain_g"))
            chain_sems = {"vector": chv, "scalar": chs, "gpsimd": chg}
            with nc.Block() as block:
                block.tensor(lambda e: rec.emit(e, "tensor", sems, chain_sems))
                block.vector(lambda e: rec.emit(ChainedEngine(e, chv), "vector", sems, chain_sems))
                block.scalar(lambda e: rec.emit(ChainedEngine(e, chs), "scalar", sems, chain_sems))
                block.gpsimd(lambda e: rec.emit(ChainedEngine(e, chg), "gpsimd", sems, chain_sems))
                block.sync(lambda e: rec.emit(e, "sync", sems, chain_sems))
        self.stk.close()
        return nc


def build_nc(**kw):
    return KB(**kw).build().finalize()


_NC_CACHE = {}


def kernel(L0_attn_x, L0_attn_t, L0_ffn_w_gate, L0_ffn_w_up, L0_ffn_w_down):
    x = np.asarray(L0_attn_x)
    if "full" not in _NC_CACHE:
        _NC_CACHE["full"] = build_nc()
    nc = _NC_CACHE["full"]
    C = _host_constants()
    base = {"t": np.ascontiguousarray(np.asarray(L0_attn_t, dtype=np.float32)),
            "wg": np.ascontiguousarray(np.asarray(L0_ffn_w_gate, dtype=np.int32)),
            "wu": np.ascontiguousarray(np.asarray(L0_ffn_w_up, dtype=np.int32)),
            "wd": np.ascontiguousarray(np.asarray(L0_ffn_w_down, dtype=np.int32))}
    base.update(C)
    in_maps = []
    for c in range(8):
        m = dict(base)
        m["xT"] = np.ascontiguousarray(x[c].T.astype(np.float32))
        in_maps.append(m)
    res = run_bass_kernel_spmd(nc, in_maps, core_ids=list(range(8)))
    outs = []
    for c in range(8):
        o = res.results[c]["out"]
        outs.append(o.reshape(DIM, P, 64).transpose(2, 1, 0).reshape(SEQ, DIM))
    return np.stack(outs, axis=0).astype(np.float32)


# version 11
# speedup vs baseline: 1.1007x; 1.1007x over previous
import numpy as np
import ml_dtypes
import concourse.bass as bass
import concourse.mybir as mybir
from concourse.bass_utils import run_bass_kernel_spmd

F32 = mybir.dt.float32
BF16 = mybir.dt.bfloat16
I32 = mybir.dt.int32
AF = mybir.ActivationFunctionType
ALU = mybir.AluOpType

SEQ, DIM, HID, KLEN = 8192, 1024, 3072, 4096
NFFT, P = 16384, 128
SCALE = float(np.float32(1.0 / 127.0))
EPS = 1e-5
SB = 512
NBLK = SEQ // SB
NCH = DIM // P
NHT = HID // P
NGRP = 8
MAGIC = 0x5F3759DF

ENGS = ("tensor", "vector", "scalar", "gpsimd", "sync")


def _host_constants():
    n = np.arange(P)
    W1 = np.exp(-2j * np.pi * np.outer(np.arange(64), np.arange(65)) / P)
    c_f1 = np.zeros((P, 130), np.float32)
    for h in (0, 1):
        c_f1[64 * h:64 * h + 64, 0:65] = W1.real
        c_f1[64 * h:64 * h + 64, 65:130] = W1.imag
    n1p = 2 * (n & 63) + (n >> 6)
    k1v = np.arange(64)
    G = np.zeros((P, 3, P, 64), np.float32)
    for k2 in range(P):
        g = (np.exp(-2j * np.pi * n1p * k2 / NFFT)[:, None]
             * np.exp(-2j * np.pi * np.outer(n1p, k1v) / P))
        G[:, 0, k2], G[:, 1, k2], G[:, 2, k2] = g.real, g.imag, -g.imag
    c_winv = np.zeros((P, 512), np.float32)
    Wi = np.exp(2j * np.pi * np.outer(k1v, n) / P)
    for h, W in ((0, Wi), (1, np.conj(Wi))):
        r = 64 * h
        c_winv[r:r + 64, 0:128] = W.real
        c_winv[r:r + 64, 128:256] = W.imag
        c_winv[r:r + 64, 256:384] = -W.imag
        c_winv[r:r + 64, 384:512] = W.real
    n2v = np.arange(64)
    Hc = np.zeros((P, P, 2, 64), np.float32)
    for k2 in range(P):
        h = ((2.0 / NFFT) * np.exp(2j * np.pi * n * k2 / NFFT)[:, None]
             * np.exp(2j * np.pi * k2 * n2v / P)[None, :])
        if k2 >= 64:
            h = np.conj(h)
        Hc[k2, :, 0], Hc[k2, :, 1] = h.real, -h.imag
    c_misc = np.zeros((P, 4), np.float32)
    c_misc[:, 0] = 1.0
    c_misc[:, 1] = np.where(n < 64, 1.0, -1.0)
    return {"c_f1": c_f1, "c_g": np.ascontiguousarray(G.reshape(P, -1)),
            "c_winv": c_winv, "c_h": np.ascontiguousarray(Hc.reshape(P, -1)),
            "c_misc": c_misc, "c_id": np.eye(P, dtype=np.float32)}


_CHAIN_OPS = ("tensor_", "scalar_tensor", "reduce_", "reciprocal", "memset",
              "memzero", "activation", "square", "sqrt", "mul", "add", "copy",
              "select", "transpose", "bn_", "sign", "engine_nop", "pool")


class ChainedEngine:

    def __init__(self, eng, sem):
        self._e = eng
        self._sem = sem
        self._k = 0

    def __getattr__(self, name):
        attr = getattr(self._e, name)
        if callable(attr) and any(name.startswith(p) for p in _CHAIN_OPS):
            def wrapped(*a, **kw):
                if self._k > 0:
                    self._e.wait_ge(self._sem, self._k)
                ins = attr(*a, **kw)
                ins.then_inc(self._sem, 1)
                self._k += 1
                return ins
            return wrapped
        return attr


class _CountStub:

    class _Ins:
        def then_inc(self, *a, **k):
            return self

    def __init__(self):
        self.k = 0

    def __getattr__(self, name):
        if any(name.startswith(p) for p in _CHAIN_OPS):
            def f(*a, **kw):
                self.k += 1
                return _CountStub._Ins()
            return f
        return lambda *a, **kw: _CountStub._Ins()


CHAINED = ("vector", "scalar", "gpsimd")


class Rec:
    def __init__(self):
        self.ops = {e: [] for e in ENGS}
        self.cnt = {}
        self.chain = {e: 0 for e in CHAINED}
        self.owner = {}
        self.mile = {}

    def n(self, sem):
        return self.cnt.setdefault(sem, 0)

    def add(self, eng, fn, waits=(), incs=()):
        wl = []
        for w in waits:
            wl.append((w[0], w[1]) if isinstance(w, tuple) else (w, self.n(w)))
        il = []
        nops = 0
        if eng in CHAINED:
            st = _CountStub()
            fn(st)
            nops = st.k
            self.chain[eng] += nops
        for s, k in incs:
            if k:
                self.cnt[s] = self.n(s) + k
                if eng in CHAINED and nops > 0:
                    self.owner.setdefault(s, set()).add(eng)
                    self.mile[(s, self.cnt[s])] = (eng, self.chain[eng])
                else:
                    assert s not in self.owner, f"sem {s} mixes chained/raw producers"
                    il.append((s, k))
        self.ops[eng].append((wl, fn, il, nops))

    def emit(self, eng_obj, name, sems, chain_sems):
        raw = eng_obj._e if isinstance(eng_obj, ChainedEngine) else eng_obj
        for wl, fn, il, nops in self.ops[name]:
            for s, v in wl:
                if v <= 0:
                    continue
                if s in self.owner:
                    need = {}
                    for vv in range(v, 0, -1):
                        e, c = self.mile[(s, vv)]
                        if e not in need:
                            need[e] = c
                        if len(need) == len(self.owner[s]):
                            break
                    for e, c in need.items():
                        raw.wait_ge(chain_sems[e], c)
                else:
                    raw.wait_ge(sems[s], v)
            k0 = eng_obj._k if isinstance(eng_obj, ChainedEngine) else 0
            ins = fn(eng_obj)
            if isinstance(eng_obj, ChainedEngine):
                assert eng_obj._k - k0 == nops, (name, nops, eng_obj._k - k0)
            for s, k in il:
                assert ins is not None
                ins = ins.then_inc(sems[s], k)


class KB:
    def __init__(self, taps=(), inject_x1=False,
                 phases=("p0a", "p0b", "p1", "p2", "p3")):
        self.taps = set(taps)
        self.inject_x1 = inject_x1
        self.phases = phases
        nc = self.nc = bass.Bass("TRN2", target_bir_lowering=False, debug=False)
        self.rec = Rec()
        self.C = _host_constants()
        self.sb_off = 17408
        self.sb_mark = []
        self.uid = 0
        d = nc.dram_tensor
        self.xT = d("xT", [DIM, SEQ], F32, kind="ExternalInput").ap()
        self.t_h = d("t", [KLEN, DIM], F32, kind="ExternalInput").ap()
        self.w_h = {"g": d("wg", [HID, DIM], I32, kind="ExternalInput").ap(),
                    "u": d("wu", [HID, DIM], I32, kind="ExternalInput").ap(),
                    "d": d("wd", [DIM, HID], I32, kind="ExternalInput").ap()}
        self.c_in = {k: d(k, list(v.shape), F32, kind="ExternalInput").ap()
                     for k, v in self.C.items()}
        self.out_h = d("out", [DIM, SEQ], F32, kind="ExternalOutput").ap()
        self.s_wnat = d("s_wnat", [2 * HID + DIM, HID], BF16, kind="Internal").ap()
        self.s_wT = {"g": d("s_wgT", [P, NCH * HID], BF16, kind="Internal").ap(),
                     "u": d("s_wuT", [P, NCH * HID], BF16, kind="Internal").ap(),
                     "d": d("s_wdT", [P, NHT * DIM], BF16, kind="Internal").ap()}
        self.s_x1 = d("s_x1", [DIM, SEQ], BF16, kind="Internal").ap()
        self.s_x2 = d("s_x2", [DIM, SEQ], BF16, kind="Internal").ap()
        self.s_ln2 = d("s_ln2", [2, SEQ], F32, kind="Internal").ap()
        self.s_rb = {t: d("s_rb_" + t, [2, SB], BF16, kind="Internal").ap() for t in ("f1", "f3")}
        self.s_that = d("s_that", [P, NGRP * 2 * SEQ], BF16, kind="Internal").ap()
        if inject_x1:
            self.x1_in = d("x1_in", [DIM, SEQ], BF16, kind="ExternalInput").ap()
            self.ln2_in = d("ln2_in", [2, SEQ], F32, kind="ExternalInput").ap()
        self.dbg = {}
        for tp, shape in (("x1", [DIM, SEQ]), ("x2", [DIM, SEQ]),
                          ("b0", [P, 2 * 65 * P]), ("yh0", [P, 2 * 64 * P]),
                          ("th0", [P, 2 * SEQ]), ("cc0", [P, 64 * 2 * P]),
                          ("win0", [P, 64 * P]), ("ln2rows", [2, SEQ])):
            if tp in self.taps:
                self.dbg[tp] = d("d_" + tp, shape, F32, kind="ExternalOutput").ap()

    def alloc(self, name, shape, dtype, align=64):
        byt = int(np.prod(shape[1:])) * (2 if dtype == BF16 else 4)
        self.sb_off = (self.sb_off + align - 1) // align * align
        t = self.nc.alloc_sbuf_tensor_at(name, list(shape), dtype, offset=self.sb_off)
        self.sb_off += byt
        assert self.sb_off <= 229000, f"SBUF overflow at {name}: {self.sb_off}"
        return t.ap()

    def mark(self):
        self.sb_mark.append(self.sb_off)

    def release(self):
        self.sb_off = self.sb_mark.pop()

    def nm(self, base):
        self.uid += 1
        return f"{base}{self.uid}"

    def barrier_waits(self):
        r = self.rec
        sems = ("thout", "s2t", "s1t", "nymmt", "nyevt", "tbin0", "ccst",
                "wTdone", "wsumev", "x2out", "x2ex", "i2x", "ptwx", "nyevx",
                "o_out", "x1out", "l2out", "dbgout", "s1ex", "i1x", "i1ex",
                "winldt", "s1et", "ptwt", "r2ld", "x1ld", "thld0", "thld1")
        return [(s, r.n(s)) for s in sems if r.n(s) > 0]

    def build(self):
        import contextlib
        nc, rec = self.nc, self.rec
        self.stk = contextlib.ExitStack()
        self.pbank = [self.stk.enter_context(nc.psum_tensor(f"pb{i}", [P, 512], F32))
                      for i in range(7)]
        self.pbf = self.stk.enter_context(nc.psum_tensor("pbf", [P, 1024], BF16))

        self.misc = self.alloc("misc", [P, 4], BF16)
        self.ident = self.alloc("ident", [P, P], BF16)
        self.tnyq = self.alloc("tnyq", [1, DIM], BF16)
        self.gsum = self.alloc("gsum", [1, HID], BF16)
        self.usum = self.alloc("usum", [1, HID], BF16)
        self.ones_row = self.alloc("ones_row", [1, P], BF16)

        rec.add("gpsimd", lambda g: g.dma_start(out=self.misc[:, :], in_=self.c_in["c_misc"][:, :]),
                incs=[("cst", 16)])
        rec.add("gpsimd", lambda g: g.dma_start(out=self.ident[:, :], in_=self.c_in["c_id"][:, :]),
                incs=[("cst", 16)])
        rec.add("vector", lambda v: v.memset(self.ones_row[:, :], 1.0), incs=[("cst1", 1)])

        if "p0b" in self.phases:
            self.conv_consts_load()
            self.p0b()
            self.release()
        if "p0a" in self.phases:
            if "p0b" not in self.phases:
                for k in "gud":
                    self.p0a_casts(k)
            self.p0a()
        if "p1" in self.phases:
            self.ffn_phase("f1")
        if self.inject_x1:
            rec.add("gpsimd", lambda g: g.dma_start(out=self.s_x1[:, :], in_=self.x1_in[:, :]),
                    incs=[("x1out", 16)])
            rec.add("gpsimd", lambda g: g.dma_start(out=self.s_ln2[:, :], in_=self.ln2_in[:, :]),
                    incs=[("l2out", 16)])
        if "p2" in self.phases:
            self.conv_consts_load()
            self.p2()
            self.release()
        if "p3" in self.phases:
            self.ffn_phase("f3")
        fw = [(s, rec.n(s)) for s in ("o_out", "x1out", "x2out", "thout", "l2out", "dbgout")
              if rec.n(s) > 0]
        rec.add("sync", lambda sy: sy.nop(), waits=fw)
        return self

    def p0a_casts(self, k):
        r0, R, Cc = {"g": (0, HID, DIM), "u": (HID, HID, DIM), "d": (2 * HID, DIM, HID)}[k]
        self.rec.add("gpsimd", lambda g, r0=r0, R=R, Cc=Cc, k=k: g.dma_start(
            out=self.s_wnat[r0:r0 + R, 0:Cc], in_=self.w_h[k][:, :]), incs=[("wnat" + k, 16)])

    def p0a(self):
        rec = self.rec
        wtmp = self.nc.alloc_sbuf_tensor_at(
            "wtmp", [P, NCH * HID], BF16, offset=229376 - NCH * HID * 2 - 64).ap()
        offs = {"g": (0, HID, DIM), "u": (HID, HID, DIM), "d": (2 * HID, DIM, HID)}
        for wi, k in enumerate("gud"):
            r0, R, Cc = offs[k]
            nch = Cc // P
            for c in range(nch):
                rec.add("sync", lambda sy, r0=r0, R=R, c=c, k=k: sy.dma_start_transpose(
                    wtmp[:, c * R:(c + 1) * R], self.s_wnat[r0:r0 + R, c * P:(c + 1) * P]),
                    waits=[("wnat" + k, 16), ("wTdone", rec.n("wTdone"))],
                    incs=[("wTsb", 16)])
            rec.add("sync", lambda sy, k=k, nch=nch, R=R: sy.dma_start(
                out=self.s_wT[k][:, 0:nch * R], in_=wtmp[:, 0:nch * R]),
                waits=[("wTsb", rec.n("wTsb"))], incs=[("wThbm", 16)])
            if k in "gu":
                row = self.gsum if k == "g" else self.usum
                pst = self.pbank[0]
                wtsb_target = rec.n("wTsb")
                for hq in range(NHT // 4):
                    for q in range(4):
                        for c in range(NCH):
                            ht = hq * 4 + q
                            rec.add("tensor", lambda t, c=c, ht=ht, q=q: t.matmul(
                                pst[0:1, q * 128:(q + 1) * 128], self.misc[:, 0:1],
                                wtmp[:, c * HID + ht * 128:c * HID + (ht + 1) * 128],
                                start=(c == 0), stop=(c == NCH - 1)),
                                waits=([("wTsb", wtsb_target), ("cst", 32),
                                        ("wsumev", rec.n("wsumev"))] if (c == 0 and q == 0) else []),
                                incs=[("wsum", 1)] if (c == NCH - 1 and q == 3) else ())
                    rec.add("vector", lambda v, row=row, hq=hq: v.tensor_copy(
                        row[0:1, hq * 512:(hq + 1) * 512], pst[0:1, 0:512]),
                        waits=[("wsum", rec.n("wsum"))], incs=[("wsumev", 1)])
            rec.add("sync", lambda sy: sy.nop(),
                    waits=[("wThbm", rec.n("wThbm")), ("wsumev", rec.n("wsumev"))],
                    incs=[("wTdone", 1)])

    def conv_consts_load(self):
        rec = self.rec
        self.mark()
        self.f1_sb = self.alloc(self.nm("f1sb"), [P, 130], BF16)
        self.g_sb = self.alloc(self.nm("gsb"), [P, 3 * P * 64], BF16)
        self.winv_sb = self.alloc(self.nm("winvsb"), [P, 512], BF16)
        self.hh_sb = self.alloc(self.nm("hhsb"), [P, P * 2 * 64], BF16)
        cw = self.barrier_waits() + [("wTdone", rec.n("wTdone")),
                                     ("ccst", rec.n("ccst"))]
        for src, dst in (("c_f1", self.f1_sb), ("c_g", self.g_sb),
                         ("c_winv", self.winv_sb), ("c_h", self.hh_sb)):
            rec.add("gpsimd", lambda g, src=src, dst=dst: g.dma_start(
                out=dst[:, :], in_=self.c_in[src][:, :]), waits=cw, incs=[("ccst", 16)])
        self.ccst_target = rec.n("ccst")

    def conv_group(self, g, mode, bufs):
        rec = self.rec
        f1_sb, g_sb, winv_sb, hh_sb = self.f1_sb, self.g_sb, self.winv_sb, self.hh_sb
        sfx = mode
        win, B = bufs["win"], bufs["B"]
        yh_re, yh_im = bufs["yhre"], bufs["yhim"]
        s_s1, s_s1e, s_s2, s_ptw = ("s1" + sfx, "s1e" + sfx, "s2" + sfx, "ptw" + sfx)
        nzrows = 32 if mode == "t" else 64
        winv = win.rearrange("p (j c) -> p j c", j=64)

        if mode == "t":
            tin = self.t_h.rearrange("(n2 j l) c -> l n2 j c", n2=32, j=64, l=2)
            for l in (0, 1):
                rec.add("gpsimd", lambda gg, l=l, g=g: gg.dma_start(
                    out=winv[64 * l:64 * l + 32, :, :],
                    in_=tin[l, :, :, g * P:(g + 1) * P]),
                    waits=[("winfree" + sfx, rec.n("winfree" + sfx))],
                    incs=[("winld" + sfx, 16)])
            wvw = ("winld" + sfx, rec.n("winld" + sfx))
        else:
            x1sl, r2n, b2n = bufs["x1sl"], bufs["r2n"], bufs["b2n"]
            x1v = x1sl.rearrange("p (n2 j l) -> p l n2 j", n2=64, j=64, l=2)
            tbanks = [self.pbf, self.pbank[3].bitcast(BF16), self.pbank[4].bitcast(BF16),
                      self.pbank[5].bitcast(BF16)]
            for jj in range(64):
                tb = tbanks[jj % 4]
                slot = (jj // 4) % 8
                pt = tb[:, slot * 128:slot * 128 + 128]
                tw = [("x1ld", rec.n("x1ld")), ("ccst", self.ccst_target), ("cst", 32)]
                if rec.n("wev" + sfx) >= 7:
                    tw.append(("wev" + sfx, rec.n("wev" + sfx) - 6))
                for l in (0, 1):
                    rec.add("tensor", lambda t, jj=jj, pt=pt, l=l: t.transpose(
                        pt[64 * l:64 * l + 64, :], x1v[:, l, :, jj], self.ident[:, :],
                        tile_position=(0, 64 * l)),
                        waits=tw if l == 0 else [], incs=[("wtr" + sfx, 1)] if l == 1 else ())
                for l in (0, 1):
                    if (jj + l) % 2 == 0:
                        rec.add("scalar", lambda s, jj=jj, l=l, pt=pt: s.activation(
                            winv[64 * l:64 * l + 64, jj, :],
                            pt[64 * l:64 * l + 64, :], AF.Identity,
                            bias=b2n[64 * l:64 * l + 64, 2 * jj + l:2 * jj + l + 1],
                            scale=r2n[64 * l:64 * l + 64, 2 * jj + l:2 * jj + l + 1]),
                            waits=[("wtr" + sfx, rec.n("wtr" + sfx))],
                            incs=[("wev" + sfx, 1)])
                    else:
                        rec.add("vector", lambda v, jj=jj, l=l, pt=pt: v.tensor_scalar(
                            winv[64 * l:64 * l + 64, jj, :],
                            pt[64 * l:64 * l + 64, :],
                            r2n[64 * l:64 * l + 64, 2 * jj + l:2 * jj + l + 1],
                            b2n[64 * l:64 * l + 64, 2 * jj + l:2 * jj + l + 1],
                            ALU.mult, ALU.add),
                            waits=[("wtr" + sfx, rec.n("wtr" + sfx))],
                            incs=[("wev" + sfx, 1)])
            wvw = ("wev" + sfx, rec.n("wev" + sfx))

        Bv = B.rearrange("p (r k c) -> p r k c", r=2, k=65)
        s1rot = [self.pbank[0], self.pbank[1], self.pbank[2], self.pbank[6]]
        for ch2 in range(64):
            pbank = s1rot[ch2 % 4]
            for ci in (0, 1):
                ch = 2 * ch2 + ci
                for l in (0, 1):
                    w = []
                    if ci == 0 and l == 0:
                        w = [wvw, ("ccst", self.ccst_target)]
                        if rec.n(s_s1e) >= 4:
                            w.append((s_s1e, rec.n(s_s1e) - 3))
                    rec.add("tensor", lambda t, ch=ch, ci=ci, l=l, pbank=pbank: t.matmul(
                        pbank[64 * l:64 * l + 64, ci * 256:ci * 256 + 130],
                        winv[64 * l:64 * l + nzrows, :, ch],
                        f1_sb[64 * l:64 * l + nzrows, :],
                        start=True, stop=True, tile_position=(64 * l, 64 * l)),
                        waits=w, incs=[(s_s1, 1)] if (ci == 1 and l == 1) else ())
            def s1ev(e, ch2=ch2, pbank=pbank):
                pin = pbank.rearrange("p (ci rk) -> p ci rk", ci=2)[:, :, 0:130]
                pin = pin.rearrange("p ci (r k) -> p ci r k", r=2)
                ov = Bv[:, :, :, 2 * ch2:2 * ch2 + 2].transpose([0, 3, 1, 2])
                if hasattr(e, "tensor_copy"):
                    return e.tensor_copy(ov, pin)
                return e.copy(ov, pin)
            if ch2 % 2 == 0:
                rec.add("vector", s1ev, waits=[(s_s1, rec.n(s_s1))], incs=[(s_s1e, 1)])
            else:
                rec.add("scalar", lambda s, ch2=ch2, pbank=pbank: s.copy(
                    Bv[:, :, :, 2 * ch2:2 * ch2 + 2].transpose([0, 3, 1, 2]),
                    pbank.rearrange("p (ci rk) -> p ci rk", ci=2)[:, :, 0:130].rearrange(
                        "p ci (r k) -> p ci r k", r=2)),
                    waits=[(s_s1, rec.n(s_s1))], incs=[(s_s1e, 1)])
        s1e_total = rec.n(s_s1e)

        pn = self.pbank[2]
        rec.add("tensor", lambda t: t.matmul(
            pn[0:1, 0:128], self.misc[:, 1:2], Bv[:, 0, 0, :], start=True, stop=True),
            waits=[(s_s1e, s1e_total), ("nyev" + sfx, rec.n("nyev" + sfx)), ("cst", 32)],
            incs=[("nymm" + sfx, 1)])
        if mode == "t":
            rec.add("vector", lambda v, g=g: v.tensor_scalar(
                self.tnyq[0:1, g * P:(g + 1) * P], pn[0:1, 0:128], 0.5, None, ALU.mult),
                waits=[("nymm" + sfx, rec.n("nymm" + sfx))], incs=[("nyev" + sfx, 1)])
        else:
            ypn, ynn, rowb = bufs["ypos"], bufs["yneg"], bufs["nyrow"]
            def nymath(v, g=g):
                v.tensor_mul(bufs["nyrowf"][0:1, :], pn[0:1, 0:128],
                             self.tnyq[0:1, g * P:(g + 1) * P])
                return v.tensor_scalar(rowb[0:1, :], bufs["nyrowf"][0:1, :],
                                       1.0 / NFFT, None, ALU.mult)
            rec.add("vector", nymath, waits=[("nymm" + sfx, rec.n("nymm" + sfx))],
                    incs=[("nyrow" + sfx, 1)])
            pnc = self.pbf[:, 0:128]
            rec.add("tensor", lambda t: t.transpose(
                pnc[:, 0:1], rowb[0:1, :], self.ident[0:1, 0:1]),
                waits=[("nyrow" + sfx, rec.n("nyrow" + sfx)),
                       ("nyev" + sfx, rec.n("nyev" + sfx))],
                incs=[("nytr" + sfx, 1)])
            def nycols(v):
                v.tensor_copy(ypn[:, 0:1], pnc[:, 0:1])
                return v.tensor_scalar(ynn[:, 0:1], pnc[:, 0:1], -1.0, None, ALU.mult)
            rec.add("vector", nycols, waits=[("nytr" + sfx, rec.n("nytr" + sfx))],
                    incs=[("nyev" + sfx, 1)])

        gv = g_sb.rearrange("p (s3 k q) -> p s3 k q", s3=3, k=P)
        if mode == "x":
            thre, thim = bufs["thre"], bufs["thim"]
            thld_base2 = [rec.n("thld0"), rec.n("thld1")]
            ptw_base = rec.n(s_ptw)
            for ck in (0, 1):
                for half, buf in ((0, thre), (1, thim)):
                    rec.add("sync", lambda sy, g=g, ck=ck, half=half, buf=buf: sy.dma_start(
                        out=buf[:, (ck % 2) * 1024:(ck % 2) * 1024 + 1024],
                        in_=self.s_that[:, g * 2 * SEQ + half * SEQ + ck * 1024:
                                        g * 2 * SEQ + half * SEQ + (ck + 1) * 1024]),
                        waits=[("thout", 16 * 2 * NGRP)] if (ck == 0 and half == 0) else [],
                        incs=[("thld%d" % (ck % 2), 16)])
        pre_rot = [self.pbank[3], self.pbank[5], self.pbank[0]]
        pim_rot = [self.pbank[4], self.pbank[6], self.pbank[1]]
        for jb in range(16):
            pre = pre_rot[jb % 3]
            pim = pim_rot[jb % 3]
            if mode == "x" and jb % 2 == 0 and jb // 2 + 2 < 8:
                ck = jb // 2 + 2
                for half, buf in ((0, bufs["thre"]), (1, bufs["thim"])):
                    rec.add("sync", lambda sy, g=g, ck=ck, half=half, buf=buf: sy.dma_start(
                        out=buf[:, (ck % 2) * 1024:(ck % 2) * 1024 + 1024],
                        in_=self.s_that[:, g * 2 * SEQ + half * SEQ + ck * 1024:
                                        g * 2 * SEQ + half * SEQ + (ck + 1) * 1024]),
                        waits=[(s_ptw, ptw_base + 2 * (ck - 1))] if half == 0 else [],
                        incs=[("thld%d" % (ck % 2), 16)])
            for jq in range(4):
                j = jb * 4 + jq
                m = 64 - j
                w0 = [(s_s1e, s1e_total)]
                rot = "xev" + sfx if mode == "x" else s_ptw
                if rec.n(rot) >= 3:
                    w0.append((rot, rec.n(rot) - 2))
                mm = [
                    (0, 0, j, 0, j, False), (0, 2, j, 1, j, True),
                    (1, 0, j, 1, j, False), (1, 1, j, 0, j, True),
                    (0, 0, 64 + j, 0, m, False), (0, 1, 64 + j, 1, m, True),
                    (1, 0, 64 + j, 1, m, False), (1, 2, 64 + j, 0, m, True),
                ]
                for (dst, gs, k2, br, bc, sp) in mm:
                    hi = k2 >= 64
                    pd = pim if dst else pre
                    rec.add("tensor", lambda t, pd=pd, gs=gs, k2=k2, br=br, bc=bc, jq=jq, hi=hi, sp=sp: t.matmul(
                        pd[64 * hi:64 * hi + 64, jq * 128:(jq + 1) * 128],
                        gv[:, gs, k2, :], Bv[:, br, bc, :],
                        start=(gs == 0), stop=sp, tile_position=(0, 64 * hi)),
                        waits=w0 if (dst == 0 and gs == 0 and not hi) else [],
                        incs=[(s_s2, 1)] if (dst == 1 and hi) and sp else ())
            if mode == "t":
                thre, thim = bufs["thre"], bufs["thim"]
                hf = (jb % 2) * 512
                def tev(e, pre=pre, pim=pim, hf=hf, jb=jb):
                    if jb == 0 and "th0" in self.dbg:
                        pass
                    if hasattr(e, "tensor_copy"):
                        e.tensor_copy(thre[:, hf:hf + 512], pre[:, :])
                        return e.tensor_copy(thim[:, hf:hf + 512], pim[:, :])
                    e.copy(thre[:, hf:hf + 512], pre[:, :])
                    return e.copy(thim[:, hf:hf + 512], pim[:, :])
                tvw = [(s_s2, rec.n(s_s2)), ("thout", rec.n("thout")),
                       ("dbgout", rec.n("dbgout"))]
                rec.add("vector" if jb % 2 == 0 else "scalar", tev, waits=tvw,
                        incs=[(s_ptw, 1)])
                if jb == 0:
                    rec.add("vector", lambda v: v.tensor_scalar(
                        thre[0:1, 0:128], thre[0:1, 0:128], 0.5, None, ALU.mult),
                        waits=[(s_ptw, rec.n(s_ptw))], incs=[("tbin0", 1)])
                if jb % 2 == 1:
                    ow = [(s_ptw, rec.n(s_ptw))]
                    if jb == 1:
                        ow.append(("tbin0", rec.n("tbin0")))
                    rec.add("sync", lambda sy, g=g, jb=jb: sy.dma_start(
                        out=self.s_that[:, g * 2 * SEQ + (jb - 1) * 512:
                                        g * 2 * SEQ + (jb + 1) * 512],
                        in_=thre[:, :]), waits=ow, incs=[("thout", 16)])
                    rec.add("sync", lambda sy, g=g, jb=jb: sy.dma_start(
                        out=self.s_that[:, g * 2 * SEQ + SEQ + (jb - 1) * 512:
                                        g * 2 * SEQ + SEQ + (jb + 1) * 512],
                        in_=thim[:, :]), incs=[("thout", 16)])
            else:
                tmp1 = bufs["ptmp1"]
                xo = (jb % 2) * 512
                xre_sb = bufs["xre"][:, xo:xo + 512]
                xim_sb = bufs["xim"][:, xo:xo + 512]
                co = ((jb // 2) % 2) * 1024 + (jb % 2) * 512
                def xev(s, pre=pre, pim=pim, xre_sb=xre_sb, xim_sb=xim_sb):
                    s.copy(xre_sb, pre[:, :])
                    return s.copy(xim_sb, pim[:, :])
                xw = [(s_s2, rec.n(s_s2))]
                if rec.n(s_ptw) >= 2:
                    xw.append((s_ptw, rec.n(s_ptw) - 1))
                rec.add("scalar", xev, waits=xw, incs=[("xev" + sfx, 1)])
                ck = jb // 2
                def ptw(v, jb=jb, co=co, xre_sb=xre_sb, xim_sb=xim_sb):
                    tr = bufs["thre"][:, co:co + 512]
                    ti = bufs["thim"][:, co:co + 512]
                    yr = yh_re[:, jb * 512:(jb + 1) * 512]
                    yi = yh_im[:, jb * 512:(jb + 1) * 512]
                    v.tensor_mul(yr, xre_sb, tr)
                    v.tensor_mul(tmp1[:, :], xim_sb, ti)
                    v.tensor_sub(yr, yr, tmp1[:, :])
                    v.tensor_mul(yi, xre_sb, ti)
                    v.tensor_mul(tmp1[:, :], xim_sb, tr)
                    return v.tensor_add(yi, yi, tmp1[:, :])
                rec.add("vector", ptw,
                        waits=[("xev" + sfx, rec.n("xev" + sfx)),
                               ("thld%d" % (ck % 2), 32 * (ck // 2 + 1) + thld_base2[ck % 2])],
                        incs=[(s_ptw, 1)])

        if mode == "t":
            rec.add("sync", lambda sy: sy.nop(), waits=[(s_s1, rec.n(s_s1))],
                    incs=[("winfree" + sfx, 1)])
            return

        Cc, x2sl, x1sl = bufs["Cc"], bufs["x2sl"], bufs["x1sl"]
        ypn, ynn = bufs["ypos"], bufs["yneg"]
        ptw_total = rec.n(s_ptw)
        yre_v = yh_re.rearrange("p (k c) -> p k c", k=64)
        yim_v = yh_im.rearrange("p (k c) -> p k c", k=64)
        ccv = Cc.rearrange("p (c r n) -> p c r n", c=64, r=2)
        hv = hh_sb.rearrange("p (n s2 q) -> p n s2 q", n=P, s2=2)
        x1v2 = x1sl.rearrange("p (n2 n1) -> p n1 n2", n2=64)
        x2v = x2sl.rearrange("p (n1 n2) -> p n1 n2", n1=P)
        i1_rot = [self.pbank[0], self.pbank[1], self.pbank[3]]
        for hh in (0, 1):
            for cp in range(32):
                pbank = i1_rot[cp % 3]
                for ci in (0, 1):
                    ch = hh * 64 + 2 * cp + ci
                    for hi in (0, 1):
                        w = []
                        if ci == 0 and hi == 0:
                            w = [(s_ptw, ptw_total), ("i2" + sfx, rec.n("i2" + sfx))]
                            if rec.n("i1e" + sfx) >= 3:
                                w.append(("i1e" + sfx, rec.n("i1e" + sfx) - 2))
                        rec.add("tensor", lambda t, ch=ch, ci=ci, hi=hi, pbank=pbank: t.matmul(
                            pbank[64 * hi:64 * hi + 64, ci * 256:(ci + 1) * 256],
                            yre_v[64 * hi:64 * hi + 64, :, ch],
                            winv_sb[64 * hi:64 * hi + 64, 0:256],
                            start=True, stop=False, tile_position=(64 * hi, 64 * hi)),
                            waits=w)
                        rec.add("tensor", lambda t, ch=ch, ci=ci, hi=hi, pbank=pbank: t.matmul(
                            pbank[64 * hi:64 * hi + 64, ci * 256:(ci + 1) * 256],
                            yim_v[64 * hi:64 * hi + 64, :, ch],
                            winv_sb[64 * hi:64 * hi + 64, 256:512],
                            start=False, stop=True, tile_position=(64 * hi, 64 * hi)),
                            incs=[("i1" + sfx, 1)] if (ci == 1 and hi == 1) else ())
                def i1ev(e, cp=cp, pbank=pbank):
                    pin = pbank.rearrange("p (ci rn) -> p ci rn", ci=2)
                    ov = ccv[:, 2 * cp:2 * cp + 2, :, :].rearrange("p c r n -> p c (r n)")
                    if hasattr(e, "tensor_copy"):
                        return e.tensor_copy(ov, pin)
                    return e.copy(ov, pin)
                rec.add("vector" if cp % 2 == 0 else "scalar", i1ev,
                        waits=[("i1" + sfx, rec.n("i1" + sfx))],
                        incs=[("i1e" + sfx, 1)])
            i1e_half = rec.n("i1e" + sfx)
            i2_rot = [self.pbank[2], self.pbank[4], self.pbank[5]]
            for nb in range(16):
                pb = i2_rot[nb % 3]
                for nq in range(8):
                    n1 = nb * 8 + nq
                    w = []
                    if nq == 0:
                        w = [("i1e" + sfx, i1e_half)]
                        if rec.n("x2e" + sfx) >= 3:
                            w.append(("x2e" + sfx, rec.n("x2e" + sfx) - 2))
                    rec.add("tensor", lambda t, n1=n1, nq=nq, hh=hh, pb=pb: t.matmul(
                        pb[64 * hh:64 * hh + 64, nq * 64:(nq + 1) * 64],
                        ccv[:, :, 0, n1], hv[:, n1, 0, :],
                        start=True, stop=False, tile_position=(0, 64 * hh)),
                        waits=w)
                    rec.add("tensor", lambda t, n1=n1, nq=nq, hh=hh, pb=pb: t.matmul(
                        pb[64 * hh:64 * hh + 64, nq * 64:(nq + 1) * 64],
                        ccv[:, :, 1, n1], hv[:, n1, 1, :],
                        start=False, stop=True, tile_position=(0, 64 * hh)),
                        incs=[("i2" + sfx, 1)] if nq == 7 else ())
                pbv = pb.rearrange("p (nq n2) -> p nq n2", nq=8)
                def x2ev(v, nb=nb, hh=hh, pbv=pbv):
                    r0 = 64 * hh
                    for par in (0, 1):
                        ny = ypn if par == 0 else ynn
                        v.scalar_tensor_tensor(
                            x2v[r0:r0 + 64, nb * 8 + par:nb * 8 + 8:2, :],
                            pbv[r0:r0 + 64, par:8:2, :], ny[r0:r0 + 64, 0:1],
                            x1v2[r0:r0 + 64, nb * 8 + par:nb * 8 + 8:2, :],
                            ALU.add, ALU.add)
                    return v.engine_nop()
                rec.add("vector", x2ev,
                        waits=[("i2" + sfx, rec.n("i2" + sfx)),
                               ("nyev" + sfx, rec.n("nyev" + sfx))],
                        incs=[("x2e" + sfx, 1)])
        rec.add("sync", lambda sy, g=g: sy.dma_start(
            out=self.s_x2[g * P:(g + 1) * P, :], in_=x2sl[:, :]),
            waits=[("x2e" + sfx, rec.n("x2e" + sfx))], incs=[("x2out", 16)])
        if "x2" in self.dbg:
            rec.add("gpsimd", lambda gg, g=g: gg.dma_start(
                out=self.dbg["x2"][g * P:(g + 1) * P, :], in_=x2sl[:, :]),
                waits=[("x2e" + sfx, rec.n("x2e" + sfx))], incs=[("dbgout", 16)])
        if "b0" in self.dbg and g == 0:
            self.dbg_dump("b0", [(B, 0, 2 * 65 * P)], [(s_s1e, s1e_total)])
        if "yh0" in self.dbg and g == 0:
            self.dbg_dump("yh0", [(yh_re, 0, 64 * P), (yh_im, 64 * P, 64 * P)],
                          [(s_ptw, ptw_total)])
        if "win0" in self.dbg and g == 0:
            self.dbg_dump("win0", [(win, 0, 64 * P)], [("wev" + sfx, rec.n("wev" + sfx))])
        if "cc0" in self.dbg and g == 0:
            self.dbg_dump("cc0", [(Cc, 0, 64 * 2 * P)], [("i1e" + sfx, rec.n("i1e" + sfx))])

    def dbg_dump(self, key, pieces, waits):
        for (buf, off, ln) in pieces:
            self.rec.add("gpsimd", lambda gg, buf=buf, off=off, ln=ln, key=key: gg.dma_start(
                out=self.dbg[key][:, off:off + ln], in_=buf[:, 0:ln]),
                waits=waits, incs=[("dbgout", 16)])

    def p0b(self):
        self.mark()
        bufs = {"win": self.alloc("t_win", [P, 64 * P], BF16),
                "B": self.alloc("t_B", [P, 2 * 65 * P], BF16),
                "yhre": None, "yhim": None,
                "thre": self.alloc("t_thre", [P, 1024], BF16),
                "thim": self.alloc("t_thim", [P, 1024], BF16)}
        for g in range(NGRP):
            if "p0a" in self.phases and g in (1, 2, 3):
                self.p0a_casts("gud"[g - 1])
            self.conv_group(g, "t", bufs)
        self.release()

    def p2(self):
        rec = self.rec
        self.mark()
        bufs = {
            "x1sl": self.alloc("x1sl", [P, SEQ], BF16),
            "win": self.alloc("p2win", [P, 64 * P], BF16),
            "B": self.alloc("p2B", [P, 2 * 65 * P], BF16),
            "yhre": self.alloc("yhre", [P, 64 * P], BF16),
            "yhim": self.alloc("yhim", [P, 64 * P], BF16),
            "thre": self.alloc("thre", [P, 2048], BF16),
            "thim": self.alloc("thim", [P, 2048], BF16),

            "r2n": self.alloc("r2n", [P, P], F32),
            "b2n": self.alloc("b2n", [P, P], F32),
            "ypos": self.alloc("ypos", [P, 1], F32),
            "yneg": self.alloc("yneg", [P, 1], F32),
            "ptmp1": self.alloc("ptmp1", [P, 512], BF16),
            "xre": self.alloc("xre", [P, 1024], BF16),
            "xim": self.alloc("xim", [P, 1024], BF16),
        }
        bufs["nyrow"] = bufs["ptmp1"][0:1, 0:P]
        bufs["nyrowf"] = bufs["xre"].bitcast(F32)[0:1, 0:P]
        bufs["Cc"] = bufs["B"][:, 0:64 * 2 * P]
        bufs["x2sl"] = bufs["win"][:, 0:SEQ]
        bufs["gtmp"] = bufs["win"][:, 4096:5120]
        l2w = [("l2out", rec.n("l2out"))] + self.barrier_waits()
        for l in (0, 1):
            rec.add("gpsimd", lambda gg, l=l: gg.dma_start(
                out=bufs["r2n"][64 * l:64 * l + 64, :],
                in_=self.s_ln2[0:1, :].rearrange("o (n2 n1) -> (o n2) n1", n2=64)),
                waits=l2w, incs=[("r2ld", 16)])
            rec.add("gpsimd", lambda gg, l=l: gg.dma_start(
                out=bufs["b2n"][64 * l:64 * l + 64, :],
                in_=self.s_ln2[1:2, :].rearrange("o (n2 n1) -> (o n2) n1", n2=64)),
                waits=l2w, incs=[("r2ld", 16)])
        pb2w = self.barrier_waits()
        for g in range(NGRP):
            rec.add("sync", lambda sy, g=g: sy.dma_start(
                out=bufs["x1sl"][:, :], in_=self.s_x1[g * P:(g + 1) * P, :]),
                waits=[("x1out", rec.n("x1out")), ("r2ld", rec.n("r2ld")),
                       ("x2ex", rec.n("x2ex")), ("x2out", rec.n("x2out")),
                       ("thout", rec.n("thout")), ("dbgout", rec.n("dbgout"))]
                      + (pb2w if g == 0 else []),
                incs=[("x1ld", 16)])
            self.conv_group(g, "x", bufs)
        self.release()

    def ffn_phase(self, tag):
        rec = self.rec
        self.mark()
        wgT = self.alloc(tag + "wgT", [P, 2 * NCH * P], BF16)
        wuT = self.alloc(tag + "wuT", [P, 2 * NCH * P], BF16)
        wdT = self.alloc(tag + "wdT", [P, NHT * DIM], BF16)
        nxs = 2
        xsl = [self.alloc(tag + f"xsl{i}", [P, NCH * SB], BF16) for i in range(nxs)]
        lnsl = [self.alloc(tag + f"lnsl{i}", [P, NCH * SB], BF16) for i in (0, 1)]
        hid = self.alloc(tag + "hid", [P, NHT * SB], BF16)
        silu = self.alloc(tag + "silu", [P, 2 * SB], BF16)
        rbp = self.alloc(tag + "rbp", [P, SB], BF16)
        rows = self.alloc(tag + "rows", [1, 6 * SB], F32)
        rows_bf = self.alloc(tag + "rowsbf", [1, 2 * SB], BF16)
        res = self.alloc(tag + "res", [P, NCH * SB], BF16 if tag == "f1" else F32)
        T = lambda s: tag + s
        f1 = tag == "f1"
        bw = self.barrier_waits()
        wgv = self.s_wT["g"].rearrange("p (c ht q) -> p ht c q", c=NCH, ht=NHT)
        wuv = self.s_wT["u"].rearrange("p (c ht q) -> p ht c q", c=NCH, ht=NHT)
        rec.add("sync", lambda sy: sy.dma_start(out=wdT[:, :], in_=self.s_wT["d"][:, :]),
                waits=[("wThbm", 48)] + bw, incs=[(T("wldd"), 16)])

        def wstream(i, ht):
            sl = ht % 2
            gw2 = list(bw) if (i == 0 and ht < 2) else []
            rec.add("sync", lambda sy, ht=ht, sl=sl: sy.dma_start(
                out=wgT[:, sl * NCH * P:(sl + 1) * NCH * P], in_=wgv[:, ht, :, :]),
                waits=[("wThbm", 48), (T("gmm"), rec.n(T("gmm")))] + gw2,
                incs=[(T("wldg%d" % sl), 16)])
            rec.add("scalar", lambda s, ht=ht, sl=sl: s.dma_start(
                out=wuT[:, sl * NCH * P:(sl + 1) * NCH * P], in_=wuv[:, ht, :, :]),
                waits=[("wThbm", 48), (T("umm"), rec.n(T("umm")))] + gw2,
                incs=[(T("wldu%d" % sl), 16)])


        pstat = self.pbank[0]
        pstat2 = self.pbank[1]
        pg = [self.pbank[2], self.pbank[3]]
        pu = [self.pbank[4], self.pbank[5]]
        pdn = [self.pbank[6], self.pbf.bitcast(F32)]
        xin = (self.xT if f1 else self.s_x2).rearrange("(c p) s -> p c s", p=P)

        def prologue(i):
            bi = i % 2
            xs = i % nxs
            lw = [("wsumev", 12 if "p0a" in self.phases else 0)]
            if i == 0:
                lw += bw
            if i >= nxs:
                lw.append((T("x1ev"), 8 * (i - nxs + 1)))
            if not f1:
                lw.append(("x2out", 16 * NGRP if "p2" in self.phases else 0))
            rec.add("gpsimd", lambda gg, i=i, xs=xs: gg.dma_start(
                out=xsl[xs].rearrange("p (c s) -> p c s", c=NCH),
                in_=xin[:, :, i * SB:(i + 1) * SB]),
                waits=[w for w in lw if w[1] > 0], incs=[(T("xld%d" % xs), 16)])
            sqw = [(T("xld%d" % xs), 16 * (i // nxs + 1)), (T("std"), rec.n(T("std")))]
            if i >= 2:
                sqw.append((T("umm"), rec.n(T("umm")) - NHT))
                if f1:
                    sqw.append((T("l2mm"), rec.n(T("l2mm"))))
            rec.add("scalar", lambda s, bi=bi, xs=xs: s.activation(
                lnsl[bi][:, :], xsl[xs][:, :], AF.Square),
                waits=sqw, incs=[(T("sqd"), 1)])
            stw = [(T("xld%d" % xs), 16 * (i // nxs + 1)), (T("row"), rec.n(T("row"))),
                   ("cst", 32), (T("l2row"), rec.n(T("l2row")))]
            if "p0a" in self.phases:
                stw.append(("wsumev", 12))
            for c in range(NCH):
                rec.add("tensor", lambda t, xs=xs, c=c: t.matmul(
                    pstat[0:1, 0:SB], self.misc[:, 0:1], xsl[xs][:, c * SB:(c + 1) * SB],
                    start=(c == 0), stop=(c == NCH - 1)),
                    waits=stw if c == 0 else [])
            for c in range(NCH):
                rec.add("tensor", lambda t, bi=bi, c=c: t.matmul(
                    pstat2[0:1, 0:SB], self.misc[:, 0:1], lnsl[bi][:, c * SB:(c + 1) * SB],
                    start=(c == 0), stop=(c == NCH - 1)),
                    waits=[(T("sqd"), rec.n(T("sqd")))] if c == 0 else [],
                    incs=[(T("std"), 1)] if c == NCH - 1 else ())
            def rowmath(v):
                s_, vv, y, tmp = (rows[0:1, k * SB:(k + 1) * SB] for k in range(4))
                v.tensor_copy(s_, pstat[0:1, 0:SB])
                v.tensor_mul(tmp, s_, s_)
                v.scalar_tensor_tensor(vv, tmp, -1.0 / DIM, pstat2[0:1, 0:SB],
                                       ALU.mult, ALU.add)
                v.tensor_scalar(vv, vv, 1.0, DIM * EPS, ALU.mult, ALU.add)
                v.tensor_single_scalar(y.bitcast(I32), vv.bitcast(I32), 1, ALU.logical_shift_right)
                v.tensor_scalar(y.bitcast(I32), y.bitcast(I32), -1, MAGIC, ALU.mult, ALU.add)
                for _ in range(4):
                    v.tensor_mul(tmp, y, y)
                    v.tensor_mul(tmp, tmp, vv)
                    v.tensor_scalar(tmp, tmp, -0.5, 1.5, ALU.mult, ALU.add)
                    v.tensor_mul(y, y, tmp)
                v.tensor_scalar(rows_bf[0:1, 0:SB], y, float(np.sqrt(DIM)), None, ALU.mult)
                return v.scalar_tensor_tensor(rows_bf[0:1, SB:2 * SB], s_,
                                              -1.0 / float(np.sqrt(DIM)), y,
                                              ALU.mult, ALU.mult)
            rmw = [(T("std"), rec.n(T("std")))]
            if f1:
                rmw.append(("l2out", rec.n("l2out")))
                if "ln2rows" in self.dbg:
                    rmw.append(("dbgout", rec.n("dbgout")))
            rec.add("vector", rowmath, waits=rmw, incs=[(T("row"), 1)])
            rec.add("sync", lambda sy, i=i: sy.dma_start(
                out=self.s_rb[tag][i % 2:i % 2 + 1, :], in_=rows_bf[0:1, 0:SB]),
                waits=[(T("row"), rec.n(T("row"))), (T("rbro"), rec.n(T("rbro")) - 16)
                       if rec.n(T("rbro")) >= 32 else (T("row"), rec.n(T("row")))],
                incs=[(T("rbro"), 16)])
        def prologue_b(i):
            bi = i % 2
            xs = i % nxs
            rec.add("gpsimd", lambda gg, i=i: gg.dma_start(
                out=rbp[:, :], in_=self.s_rb[tag][i % 2:i % 2 + 1, :].broadcast_to([P, SB])),
                waits=[(T("rbro"), 16 * (i + 1)), (T("lnap"), rec.n(T("lnap")))],
                incs=[(T("rbld"), 16)])
            for c in range(NCH):
                rec.add("vector", lambda v, bi=bi, xs=xs, c=c: v.tensor_mul(
                    lnsl[bi][:, c * SB:(c + 1) * SB], xsl[xs][:, c * SB:(c + 1) * SB], rbp[:, :]),
                    waits=[(T("rbld"), rec.n(T("rbld"))), (T("std"), rec.n(T("std")))] if c == 0 else [],
                    incs=[(T("lnap"), 1)])

        def gate_up(i):
            bi = i % 2
            for ht in range(NHT):
                q = ht % 2
                sl = ht % 2
                nld = 16 * (i * NHT + ht + 1 + 1) if (i * NHT + ht + 2 <= NBLK * NHT) else 16 * NBLK * NHT
                gw = [(T("lnap"), rec.n(T("lnap"))),
                      (T("wldg%d" % sl), 16 * ((i * NHT + ht) // 2 + 1))]
                if rec.n(T("silu")) >= 2:
                    gw.append((T("silu"), rec.n(T("silu")) - 1))
                for c in range(NCH):
                    if c == NCH - 1:
                        rec.add("tensor", lambda t, ht=ht, q=q: t.matmul(
                            pg[q][:, 0:SB], self.gsum[0:1, ht * P:(ht + 1) * P],
                            rows_bf[0:1, SB:2 * SB], start=False, stop=False,
                            skip_group_check=True))
                    rec.add("tensor", lambda t, c=c, sl=sl, q=q: t.matmul(
                        pg[q][:, 0:SB], wgT[:, sl * NCH * P + c * P:sl * NCH * P + (c + 1) * P],
                        lnsl[i % 2][:, c * SB:(c + 1) * SB], start=(c == 0),
                        stop=(c == NCH - 1)),
                        waits=gw if c == 0 else [],
                        incs=[(T("gmm"), 1)] if c == NCH - 1 else ())
                uw = [(T("wldu%d" % sl), 16 * ((i * NHT + ht) // 2 + 1))]
                if rec.n(T("hmul")) >= 2:
                    uw.append((T("hmul"), rec.n(T("hmul")) - 1))
                for c in range(NCH):
                    if c == NCH - 1:
                        rec.add("tensor", lambda t, ht=ht, q=q: t.matmul(
                            pu[q][:, 0:SB], self.usum[0:1, ht * P:(ht + 1) * P],
                            rows_bf[0:1, SB:2 * SB], start=False, stop=False,
                            skip_group_check=True))
                    rec.add("tensor", lambda t, c=c, sl=sl, q=q: t.matmul(
                        pu[q][:, 0:SB], wuT[:, sl * NCH * P + c * P:sl * NCH * P + (c + 1) * P],
                        lnsl[i % 2][:, c * SB:(c + 1) * SB], start=(c == 0),
                        stop=(c == NCH - 1)),
                        waits=uw if c == 0 else [],
                        incs=[(T("umm"), 1)] if c == NCH - 1 else ())
                nxt = i * NHT + ht + 2
                if nxt < NBLK * NHT:
                    wstream(nxt // NHT, nxt % NHT)
                sw = [(T("gmm"), rec.n(T("gmm")))]
                if rec.n(T("hmul")) >= 1:
                    sw.append((T("hmul"), rec.n(T("hmul")) - 1))
                rec.add("scalar", lambda s, ht=ht, q=q: s.activation(
                    silu[:, q * SB:(q + 1) * SB], pg[q][:, 0:SB], AF.Silu, scale=SCALE),
                    waits=sw, incs=[(T("silu"), 1)])
                hw = [(T("silu"), rec.n(T("silu"))), (T("umm"), rec.n(T("umm")))]
                if i > 0 and ht == 0:
                    hw.append((T("dmm"), rec.n(T("dmm"))))
                rec.add("vector", lambda v, ht=ht, q=q: v.tensor_mul(
                    hid[:, ht * SB:(ht + 1) * SB], silu[:, q * SB:(q + 1) * SB], pu[q][:, 0:SB]),
                    waits=hw, incs=[(T("hmul"), 1)])


        def down_store(i):
            bi = i % 2
            xs = i % nxs
            hm = rec.n(T("hmul"))
            for dt in range(NCH):
                pd = pdn[dt % 2]
                dw = []
                if dt == 0:
                    dw = [(T("hmul"), hm), (T("wldd"), 16)]
                    if i > 0:
                        dw.append((("x1out" if f1 else "o_out"), 16 * i))
                    if f1:
                        dw.append((T("l2row"), rec.n(T("l2row"))))
                    if "x1" in self.dbg and f1:
                        dw.append(("dbgout", rec.n("dbgout")))
                if dt >= 2:
                    dw.append((T("sdev"), rec.n(T("sdev")) - 1))
                for hc in range(NHT):
                    rec.add("tensor", lambda t, dt=dt, hc=hc, pd=pd: t.matmul(
                        pd[:, 0:SB],
                        wdT[:, hc * DIM + dt * P:hc * DIM + (dt + 1) * P],
                        hid[:, hc * SB:(hc + 1) * SB], start=(hc == 0), stop=(hc == NHT - 1)),
                        waits=dw if hc == 0 else [],
                        incs=[(T("dmm"), 1)] if hc == NHT - 1 else ())
                rec.add("scalar", lambda s, dt=dt, pd=pd: s.activation(
                    res[:, dt * SB:(dt + 1) * SB], pd[:, 0:SB], AF.Copy,
                    scale=0.5 * SCALE * SCALE),
                    waits=[(T("dmm"), rec.n(T("dmm")))],
                    incs=[(T("sdev"), 1)])
                rec.add("vector", lambda v, xs=xs, dt=dt: v.tensor_add(
                    res[:, dt * SB:(dt + 1) * SB], res[:, dt * SB:(dt + 1) * SB],
                    xsl[xs][:, dt * SB:(dt + 1) * SB]),
                    waits=[(T("sdev"), rec.n(T("sdev")))],
                    incs=[(T("x1ev"), 1)])
            if f1:
                x1o = self.s_x1.rearrange("(c p) s -> p c s", p=P)
                rec.add("sync", lambda sy, i=i: sy.dma_start(
                    out=x1o[:, :, i * SB:(i + 1) * SB],
                    in_=res.rearrange("p (c s) -> p c s", c=NCH)),
                    waits=[(T("x1ev"), rec.n(T("x1ev"))), ("x1out", 16 * i)],
                    incs=[("x1out", 16)])
                if "x1" in self.dbg:
                    d1 = self.dbg["x1"].rearrange("(c p) s -> p c s", p=P)
                    rec.add("gpsimd", lambda gg, i=i: gg.dma_start(
                        out=d1[:, :, i * SB:(i + 1) * SB],
                        in_=res.rearrange("p (c s) -> p c s", c=NCH)),
                        waits=[(T("x1ev"), rec.n(T("x1ev")))], incs=[("dbgout", 16)])
            else:
                oo = self.out_h.rearrange("(c p) s -> p c s", p=P)
                rec.add("sync", lambda sy, i=i: sy.dma_start(
                    out=oo[:, :, i * SB:(i + 1) * SB],
                    in_=res.rearrange("p (c s) -> p c s", c=NCH)),
                    waits=[(T("x1ev"), rec.n(T("x1ev"))), ("o_out", 16 * i)],
                    incs=[("o_out", 16)])
            if f1:
                rec.add("scalar", lambda s, bi=bi: s.activation(
                    lnsl[bi][:, :], res[:, :], AF.Square),
                    waits=[(T("x1ev"), rec.n(T("x1ev"))), (T("l2mm"), rec.n(T("l2mm"))),
                           (T("umm"), rec.n(T("umm")))],
                    incs=[(T("l2sq"), 1)])
                l2w0 = [(T("x1ev"), rec.n(T("x1ev"))), (T("l2row"), rec.n(T("l2row"))),
                        (T("row"), rec.n(T("row")))]
                for c in range(NCH):
                    rec.add("tensor", lambda t, c=c: t.matmul(
                        pstat[0:1, 0:SB], self.misc[:, 0:1],
                        res[:, c * SB:(c + 1) * SB], start=(c == 0), stop=(c == NCH - 1)),
                        waits=l2w0 if c == 0 else [])
                for c in range(NCH):
                    rec.add("tensor", lambda t, bi=bi, c=c: t.matmul(
                        pstat2[0:1, 0:SB], self.misc[:, 0:1],
                        lnsl[bi][:, c * SB:(c + 1) * SB], start=(c == 0), stop=(c == NCH - 1)),
                        waits=[(T("l2sq"), rec.n(T("l2sq")))] if c == 0 else [],
                        incs=[(T("l2mm"), 1)] if c == NCH - 1 else ())
                def row2(v):
                    s_, vv, y, tmp = (rows[0:1, k * SB:(k + 1) * SB] for k in range(4))
                    r2o = rows[0:1, 2 * SB:3 * SB]
                    b2 = rows[0:1, 5 * SB:6 * SB]
                    v.tensor_copy(s_, pstat[0:1, 0:SB])
                    v.tensor_mul(tmp, s_, s_)
                    v.scalar_tensor_tensor(vv, tmp, -1.0 / DIM, pstat2[0:1, 0:SB],
                                           ALU.mult, ALU.add)
                    v.tensor_scalar(vv, vv, 1.0, DIM * EPS, ALU.mult, ALU.add)
                    v.tensor_single_scalar(y.bitcast(I32), vv.bitcast(I32), 1, ALU.logical_shift_right)
                    v.tensor_scalar(y.bitcast(I32), y.bitcast(I32), -1, MAGIC, ALU.mult, ALU.add)
                    for _ in range(4):
                        v.tensor_mul(tmp, y, y)
                        v.tensor_mul(tmp, tmp, vv)
                        v.tensor_scalar(tmp, tmp, -0.5, 1.5, ALU.mult, ALU.add)
                        v.tensor_mul(y, y, tmp)
                    v.scalar_tensor_tensor(b2, s_, -1.0 / float(np.sqrt(DIM)), y,
                                           ALU.mult, ALU.mult)
                    return v.tensor_scalar(r2o, y, float(np.sqrt(DIM)), None, ALU.mult)
                rec.add("vector", row2, waits=[(T("l2mm"), rec.n(T("l2mm"))),
                                               ("l2out", rec.n("l2out"))],
                        incs=[(T("l2row"), 1)])
                rec.add("sync", lambda sy, i=i: sy.dma_start(
                    out=self.s_ln2[0:1, i * SB:(i + 1) * SB], in_=rows[0:1, 2 * SB:3 * SB]),
                    waits=[(T("l2row"), rec.n(T("l2row"))), ("l2out", rec.n("l2out"))],
                    incs=[("l2out", 16)])
                rec.add("sync", lambda sy, i=i: sy.dma_start(
                    out=self.s_ln2[1:2, i * SB:(i + 1) * SB], in_=rows[0:1, 5 * SB:6 * SB]),
                    incs=[("l2out", 16)])
                if "ln2rows" in self.dbg:
                    rec.add("gpsimd", lambda gg, i=i: gg.dma_start(
                        out=self.dbg["ln2rows"][0:1, i * SB:(i + 1) * SB],
                        in_=rows[0:1, 2 * SB:3 * SB]),
                        waits=[(T("l2row"), rec.n(T("l2row")))], incs=[("dbgout", 16)])
                    rec.add("gpsimd", lambda gg, i=i: gg.dma_start(
                        out=self.dbg["ln2rows"][1:2, i * SB:(i + 1) * SB],
                        in_=rows[0:1, 5 * SB:6 * SB]),
                        incs=[("dbgout", 16)])

        wstream(0, 0)
        wstream(0, 1)
        prologue(0)
        prologue_b(0)
        for i in range(NBLK):
            gate_up(i)
            if i + 1 < NBLK:
                prologue(i + 1)
                prologue_b(i + 1)
            down_store(i)
        self.release()

    def finalize(self):
        import contextlib
        nc, rec = self.nc, self.rec
        wnames = set()
        for e in ENGS:
            for wl, _, _, _ in rec.ops[e]:
                for s, v in wl:
                    wnames.add(s)
        names = sorted(set(rec.cnt.keys()) | wnames)
        assert len(names) <= 200, len(names)
        with contextlib.ExitStack() as sx:
            names = [s for s in names if s not in rec.owner]
            sems = {s: sx.enter_context(nc.semaphore(s)) for s in names}
            chv = sx.enter_context(nc.semaphore("chain_v"))
            chs = sx.enter_context(nc.semaphore("chain_s"))
            chg = sx.enter_context(nc.semaphore("chain_g"))
            chain_sems = {"vector": chv, "scalar": chs, "gpsimd": chg}
            with nc.Block() as block:
                block.tensor(lambda e: rec.emit(e, "tensor", sems, chain_sems))
                block.vector(lambda e: rec.emit(ChainedEngine(e, chv), "vector", sems, chain_sems))
                block.scalar(lambda e: rec.emit(ChainedEngine(e, chs), "scalar", sems, chain_sems))
                block.gpsimd(lambda e: rec.emit(ChainedEngine(e, chg), "gpsimd", sems, chain_sems))
                block.sync(lambda e: rec.emit(e, "sync", sems, chain_sems))
        self.stk.close()
        return nc


def build_nc(**kw):
    return KB(**kw).build().finalize()


_NC_CACHE = {}


def kernel(L0_attn_x, L0_attn_t, L0_ffn_w_gate, L0_ffn_w_up, L0_ffn_w_down):
    x = np.asarray(L0_attn_x)
    if "full" not in _NC_CACHE:
        _NC_CACHE["full"] = build_nc()
    nc = _NC_CACHE["full"]
    C = _host_constants()
    base = {"t": np.ascontiguousarray(np.asarray(L0_attn_t, dtype=np.float32)),
            "wg": np.ascontiguousarray(np.asarray(L0_ffn_w_gate, dtype=np.int32)),
            "wu": np.ascontiguousarray(np.asarray(L0_ffn_w_up, dtype=np.int32)),
            "wd": np.ascontiguousarray(np.asarray(L0_ffn_w_down, dtype=np.int32))}
    base.update(C)
    in_maps = []
    for c in range(8):
        m = dict(base)
        m["xT"] = np.ascontiguousarray(x[c].T.astype(np.float32))
        in_maps.append(m)
    res = run_bass_kernel_spmd(nc, in_maps, core_ids=list(range(8)))
    outs = []
    for c in range(8):
        o = res.results[c]["out"]
        outs.append(o.reshape(DIM, P, 64).transpose(2, 1, 0).reshape(SEQ, DIM))
    return np.stack(outs, axis=0).astype(np.float32)


# version 16
# speedup vs baseline: 1.1533x; 1.0478x over previous
import numpy as np
import ml_dtypes
import concourse.bass as bass
import concourse.mybir as mybir
from concourse.bass_utils import run_bass_kernel_spmd

F32 = mybir.dt.float32
BF16 = mybir.dt.bfloat16
I32 = mybir.dt.int32
AF = mybir.ActivationFunctionType
ALU = mybir.AluOpType

SEQ, DIM, HID, KLEN = 8192, 1024, 3072, 4096
NFFT, P = 16384, 128
SCALE = float(np.float32(1.0 / 127.0))
EPS = 1e-5
SB = 512
NBLK = SEQ // SB
NCH = DIM // P
NHT = HID // P
NGRP = 8

ENGS = ("tensor", "vector", "scalar", "gpsimd", "sync")


def _host_constants():
    n = np.arange(P)
    W1 = np.exp(-2j * np.pi * np.outer(np.arange(64), np.arange(65)) / P)
    c_f1 = np.zeros((P, 130), np.float32)
    for h in (0, 1):
        c_f1[64 * h:64 * h + 64, 0:65] = W1.real
        c_f1[64 * h:64 * h + 64, 65:130] = W1.imag
    n1p = 2 * (n & 63) + (n >> 6)
    k1v = np.arange(64)
    G = np.zeros((P, 3, P, 64), np.float32)
    for k2 in range(P):
        g = (np.exp(-2j * np.pi * n1p * k2 / NFFT)[:, None]
             * np.exp(-2j * np.pi * np.outer(n1p, k1v) / P))
        G[:, 0, k2], G[:, 1, k2], G[:, 2, k2] = g.real, g.imag, -g.imag
    c_winv = np.zeros((P, 512), np.float32)
    Wi = np.exp(2j * np.pi * np.outer(k1v, n) / P)
    for h, W in ((0, Wi), (1, np.conj(Wi))):
        r = 64 * h
        c_winv[r:r + 64, 0:128] = W.real
        c_winv[r:r + 64, 128:256] = W.imag
        c_winv[r:r + 64, 256:384] = -W.imag
        c_winv[r:r + 64, 384:512] = W.real
    n2v = np.arange(64)
    Hc = np.zeros((P, P, 2, 64), np.float32)
    for k2 in range(P):
        h = ((2.0 / NFFT) * np.exp(2j * np.pi * n * k2 / NFFT)[:, None]
             * np.exp(2j * np.pi * k2 * n2v / P)[None, :])
        if k2 >= 64:
            h = np.conj(h)
        Hc[k2, :, 0], Hc[k2, :, 1] = h.real, -h.imag
    c_misc = np.zeros((P, 4), np.float32)
    c_misc[:, 0] = 1.0
    c_misc[:, 1] = np.where(n < 64, 1.0, -1.0)
    return {"c_f1": c_f1, "c_g": np.ascontiguousarray(G.reshape(P, -1)),
            "c_winv": c_winv, "c_h": np.ascontiguousarray(Hc.reshape(P, -1)),
            "c_misc": c_misc, "c_id": np.eye(P, dtype=np.float32)}


_CHAIN_OPS = ("tensor_", "scalar_tensor", "reduce_", "reciprocal", "memset",
              "memzero", "activation", "square", "sqrt", "mul", "add", "copy",
              "select", "transpose", "bn_", "sign", "engine_nop", "pool")


class ChainedEngine:

    def __init__(self, eng, sem):
        self._e = eng
        self._sem = sem
        self._k = 0

    def __getattr__(self, name):
        attr = getattr(self._e, name)
        if callable(attr) and any(name.startswith(p) for p in _CHAIN_OPS):
            def wrapped(*a, **kw):
                if self._k > 0:
                    self._e.wait_ge(self._sem, self._k)
                ins = attr(*a, **kw)
                ins.then_inc(self._sem, 1)
                self._k += 1
                return ins
            return wrapped
        return attr


class _CountStub:

    class _Ins:
        def then_inc(self, *a, **k):
            return self

    def __init__(self):
        self.k = 0

    def __getattr__(self, name):
        if any(name.startswith(p) for p in _CHAIN_OPS):
            def f(*a, **kw):
                self.k += 1
                return _CountStub._Ins()
            return f
        return lambda *a, **kw: _CountStub._Ins()


CHAINED = ("vector", "scalar", "gpsimd")


class Rec:
    def __init__(self):
        self.ops = {e: [] for e in ENGS}
        self.cnt = {}
        self.chain = {e: 0 for e in CHAINED}
        self.owner = {}
        self.mile = {}

    def n(self, sem):
        return self.cnt.setdefault(sem, 0)

    def add(self, eng, fn, waits=(), incs=()):
        wl = []
        for w in waits:
            wl.append((w[0], w[1]) if isinstance(w, tuple) else (w, self.n(w)))
        il = []
        nops = 0
        if eng in CHAINED:
            st = _CountStub()
            fn(st)
            nops = st.k
            self.chain[eng] += nops
        for s, k in incs:
            if k:
                self.cnt[s] = self.n(s) + k
                if eng in CHAINED and nops > 0:
                    self.owner.setdefault(s, set()).add(eng)
                    self.mile[(s, self.cnt[s])] = (eng, self.chain[eng])
                else:
                    assert s not in self.owner, f"sem {s} mixes chained/raw producers"
                    il.append((s, k))
        self.ops[eng].append((wl, fn, il, nops))

    def emit(self, eng_obj, name, sems, chain_sems):
        raw = eng_obj._e if isinstance(eng_obj, ChainedEngine) else eng_obj
        for wl, fn, il, nops in self.ops[name]:
            for s, v in wl:
                if v <= 0:
                    continue
                if s in self.owner:
                    need = {}
                    for vv in range(v, 0, -1):
                        e, c = self.mile[(s, vv)]
                        if e not in need:
                            need[e] = c
                        if len(need) == len(self.owner[s]):
                            break
                    for e, c in need.items():
                        raw.wait_ge(chain_sems[e], c)
                else:
                    raw.wait_ge(sems[s], v)
            k0 = eng_obj._k if isinstance(eng_obj, ChainedEngine) else 0
            ins = fn(eng_obj)
            if isinstance(eng_obj, ChainedEngine):
                assert eng_obj._k - k0 == nops, (name, nops, eng_obj._k - k0)
            for s, k in il:
                assert ins is not None
                ins = ins.then_inc(sems[s], k)


class KB:
    def __init__(self, taps=(), inject_x1=False,
                 phases=("p0a", "p0b", "p1", "p2", "p3")):
        self.taps = set(taps)
        self.inject_x1 = inject_x1
        self.phases = phases
        nc = self.nc = bass.Bass("TRN2", target_bir_lowering=False, debug=False)
        self.rec = Rec()
        self.C = _host_constants()
        self.sb_off = 17408
        self.sb_mark = []
        self.uid = 0
        d = nc.dram_tensor
        self.xT = d("xT", [DIM, SEQ], F32, kind="ExternalInput").ap()
        self.t_h = d("t", [KLEN, DIM], F32, kind="ExternalInput").ap()
        self.w_h = {"g": d("wg", [HID, DIM], I32, kind="ExternalInput").ap(),
                    "u": d("wu", [HID, DIM], I32, kind="ExternalInput").ap(),
                    "d": d("wd", [DIM, HID], I32, kind="ExternalInput").ap()}
        self.c_in = {k: d(k, list(v.shape), F32, kind="ExternalInput").ap()
                     for k, v in self.C.items()}
        self.out_h = d("out", [DIM, SEQ], F32, kind="ExternalOutput").ap()
        self.s_wnat = d("s_wnat", [2 * HID + DIM, HID], BF16, kind="Internal").ap()
        self.s_wT = {"g": d("s_wgT", [P, NCH * HID], BF16, kind="Internal").ap(),
                     "u": d("s_wuT", [P, NCH * HID], BF16, kind="Internal").ap(),
                     "d": d("s_wdT", [P, NHT * DIM], BF16, kind="Internal").ap()}
        self.s_x1 = d("s_x1", [DIM, SEQ], BF16, kind="Internal").ap()
        self.s_x2 = d("s_x2", [DIM, SEQ], BF16, kind="Internal").ap()
        self.s_ln2 = d("s_ln2", [2, SEQ], F32, kind="Internal").ap()
        self.s_rb = {t: d("s_rb_" + t, [2, 2 * SB], BF16, kind="Internal").ap() for t in ("f1", "f3")}
        self.s_that = d("s_that", [P, NGRP * 2 * SEQ], BF16, kind="Internal").ap()
        if inject_x1:
            self.x1_in = d("x1_in", [DIM, SEQ], BF16, kind="ExternalInput").ap()
            self.ln2_in = d("ln2_in", [2, SEQ], F32, kind="ExternalInput").ap()
        self.dbg = {}
        for tp, shape in (("x1", [DIM, SEQ]), ("x2", [DIM, SEQ]),
                          ("b0", [P, 2 * 65 * P]), ("yh0", [P, 2 * 64 * P]),
                          ("th0", [P, 2 * SEQ]), ("cc0", [P, 64 * 2 * P]),
                          ("win0", [P, 64 * P]), ("ln2rows", [2, SEQ])):
            if tp in self.taps:
                self.dbg[tp] = d("d_" + tp, shape, F32, kind="ExternalOutput").ap()

    def alloc(self, name, shape, dtype, align=64):
        byt = int(np.prod(shape[1:])) * (2 if dtype == BF16 else 4)
        self.sb_off = (self.sb_off + align - 1) // align * align
        t = self.nc.alloc_sbuf_tensor_at(name, list(shape), dtype, offset=self.sb_off)
        self.sb_off += byt
        assert self.sb_off <= 229000, f"SBUF overflow at {name}: {self.sb_off}"
        return t.ap()

    def mark(self):
        self.sb_mark.append(self.sb_off)

    def release(self):
        self.sb_off = self.sb_mark.pop()

    def nm(self, base):
        self.uid += 1
        return f"{base}{self.uid}"

    def barrier_waits(self):
        r = self.rec
        sems = ("thout", "s2t", "s1t", "nymmt", "nyevt", "tbin0", "ccall",
                "cc_f1", "cc_g", "cc_wv", "cc_hh",
                "wTdone", "x2out", "x2ex", "i2x", "ptwx", "nyevx",
                "o_out", "x1out", "l2out", "dbgout", "s1ex", "i1x", "i1ex",
                "winld0t", "winld1t", "s1et", "ptwt", "r2ld", "x1ld", "thld0", "thld1")
        return [(s, r.n(s)) for s in sems if r.n(s) > 0]

    def build(self):
        import contextlib
        nc, rec = self.nc, self.rec
        self.stk = contextlib.ExitStack()
        self.pbank = [self.stk.enter_context(nc.psum_tensor(f"pb{i}", [P, 512], F32))
                      for i in range(7)]
        self.pbf = self.stk.enter_context(nc.psum_tensor("pbf", [P, 1024], BF16))

        self.misc = self.alloc("misc", [P, 4], BF16)
        self.ident = self.alloc("ident", [P, P], BF16)
        self.tnyq = self.alloc("tnyq", [1, DIM], BF16)

        rec.add("gpsimd", lambda g: g.dma_start(out=self.misc[:, :], in_=self.c_in["c_misc"][:, :]),
                incs=[("cst", 16)])
        rec.add("gpsimd", lambda g: g.dma_start(out=self.ident[:, :], in_=self.c_in["c_id"][:, :]),
                incs=[("cst", 16)])

        if "p0b" in self.phases:
            if "p0a" in self.phases:
                for k in "gud":
                    self.p0a_casts(k)
            self.conv_consts_load()
            self.p0b()
            self.release()
        if "p0a" in self.phases:
            if "p0b" not in self.phases:
                for k in "gud":
                    self.p0a_casts(k)
            self.p0a()
        if "p1" in self.phases:
            self.ffn_phase("f1")
        if self.inject_x1:
            rec.add("gpsimd", lambda g: g.dma_start(out=self.s_x1[:, :], in_=self.x1_in[:, :]),
                    incs=[("x1out", 16)])
            rec.add("gpsimd", lambda g: g.dma_start(out=self.s_ln2[:, :], in_=self.ln2_in[:, :]),
                    incs=[("l2out", 16)])
        if "p2" in self.phases:
            self.conv_consts_load()
            self.p2()
            self.release()
        if "p3" in self.phases:
            self.ffn_phase("f3")
        fw = [(s, rec.n(s)) for s in ("o_out", "x1out", "x2out", "thout", "l2out", "dbgout")
              if rec.n(s) > 0]
        rec.add("sync", lambda sy: sy.nop(), waits=fw)
        return self

    def p0a_casts(self, k):
        r0, R, Cc = {"g": (0, HID, DIM), "u": (HID, HID, DIM), "d": (2 * HID, DIM, HID)}[k]
        self.rec.add("gpsimd", lambda g, r0=r0, R=R, Cc=Cc, k=k: g.dma_start(
            out=self.s_wnat[r0:r0 + R, 0:Cc], in_=self.w_h[k][:, :]), incs=[("wnat" + k, 16)])

    def p0a(self):
        rec = self.rec
        wtmp = self.nc.alloc_sbuf_tensor_at(
            "wtmp", [P, NCH * HID], BF16, offset=229376 - NCH * HID * 2 - 64).ap()
        offs = {"g": (0, HID, DIM), "u": (HID, HID, DIM), "d": (2 * HID, DIM, HID)}
        for wi, k in enumerate("gud"):
            r0, R, Cc = offs[k]
            nch = Cc // P
            for c in range(nch):
                rec.add("sync", lambda sy, r0=r0, R=R, c=c, k=k: sy.dma_start_transpose(
                    wtmp[:, c * R:(c + 1) * R], self.s_wnat[r0:r0 + R, c * P:(c + 1) * P]),
                    waits=[("wnat" + k, 16), ("wTdone", rec.n("wTdone"))],
                    incs=[("wTsb", 16)])
            rec.add("sync", lambda sy, k=k, nch=nch, R=R: sy.dma_start(
                out=self.s_wT[k][:, 0:nch * R], in_=wtmp[:, 0:nch * R]),
                waits=[("wTsb", rec.n("wTsb"))], incs=[("wThbm" + k, 16)])
            rec.add("sync", lambda sy: sy.nop(),
                    waits=[("wThbm" + k, rec.n("wThbm" + k))],
                    incs=[("wTdone", 1)])

    def conv_consts_load(self):
        rec = self.rec
        self.mark()
        self.f1_sb = self.alloc(self.nm("f1sb"), [P, 130], BF16)
        self.g_sb = self.alloc(self.nm("gsb"), [P, 3 * P * 64], BF16)
        self.winv_sb = self.alloc(self.nm("winvsb"), [P, 512], BF16)
        self.hh_sb = self.alloc(self.nm("hhsb"), [P, P * 2 * 64], BF16)
        cw = self.barrier_waits() + [("wTdone", rec.n("wTdone"))] + \
            [("cc_" + s, rec.n("cc_" + s)) for s in ("f1", "g", "wv", "hh")]
        for src, sem, dst in (("c_f1", "f1", self.f1_sb), ("c_g", "g", self.g_sb),
                              ("c_winv", "wv", self.winv_sb), ("c_h", "hh", self.hh_sb)):
            rec.add("gpsimd", lambda g, src=src, dst=dst: g.dma_start(
                out=dst[:, :], in_=self.c_in[src][:, :]), waits=cw,
                incs=[("cc_" + sem, 16)])
        self.cc_tgt = {s: rec.n("cc_" + s) for s in ("f1", "g", "wv", "hh")}
        rec.add("sync", lambda sy: sy.nop(),
                waits=[("cc_" + s, self.cc_tgt[s]) for s in ("f1", "g", "wv", "hh")],
                incs=[("ccall", 1)])
        self.ccst_target = rec.n("ccall")

    def conv_group(self, g, mode, bufs):
        rec = self.rec
        f1_sb, g_sb, winv_sb, hh_sb = self.f1_sb, self.g_sb, self.winv_sb, self.hh_sb
        sfx = mode
        win, B = bufs["win"], bufs["B"]
        yh_re, yh_im = bufs["yhre"], bufs["yhim"]
        s_s1, s_s1e, s_s2, s_ptw = ("s1" + sfx, "s1e" + sfx, "s2" + sfx, "ptw" + sfx)
        nzrows = 32 if mode == "t" else 64
        winv = win.rearrange("p (j c) -> p j c", j=64)

        if mode == "t":
            tin = self.t_h.rearrange("(n2 j l) c -> l n2 j c", n2=32, j=64, l=2)
            wsem = "winld%d" % (g % 2) + sfx
            for l in (0, 1):
                rec.add("gpsimd", lambda gg, l=l, g=g: gg.dma_start(
                    out=winv[64 * l:64 * l + 32, :, :],
                    in_=tin[l, :, :, g * P:(g + 1) * P]),
                    waits=[(s_s1, max(0, rec.n(s_s1) - 64))],
                    incs=[(wsem, 16)])
            wvw = (wsem, rec.n(wsem))
        else:
            x1sl, r2n, b2n = bufs["x1sl"], bufs["r2m"], bufs["b2m"]
            x1v = x1sl.rearrange("p (n2 j l) -> p l n2 j", n2=64, j=64, l=2)
            tbanks = [self.pbf, self.pbank[3].bitcast(BF16), self.pbank[4].bitcast(BF16),
                      self.pbank[5].bitcast(BF16)]
            for jj in range(64):
                tb = tbanks[jj % 4]
                slot = (jj // 4) % 8
                pt = tb[:, slot * 128:slot * 128 + 128]
                tw = [("x1ld", rec.n("x1ld")), ("ccall", self.ccst_target), ("cst", 32)]
                if rec.n("wev" + sfx) >= 4:
                    tw.append(("wev" + sfx, rec.n("wev" + sfx) - 3))
                for l in (0, 1):
                    rec.add("tensor", lambda t, jj=jj, pt=pt, l=l: t.transpose(
                        pt[64 * l:64 * l + 64, :], x1v[:, l, :, jj], self.ident[:, :],
                        tile_position=(0, 64 * l)),
                        waits=tw if l == 0 else [], incs=[("wtr" + sfx, 1)] if l == 1 else ())
                if jj % 2 == 0:
                    rec.add("scalar", lambda s, jj=jj, pt=pt: s.activation(
                        winv[:, jj, :], pt[:, :], AF.Identity,
                        bias=b2n[:, jj:jj + 1], scale=r2n[:, jj:jj + 1]),
                        waits=[("wtr" + sfx, rec.n("wtr" + sfx)), ("r2m", 4)],
                        incs=[("wev" + sfx, 1)])
                else:
                    rec.add("vector", lambda v, jj=jj, pt=pt: v.tensor_scalar(
                        winv[:, jj, :], pt[:, :],
                        r2n[:, jj:jj + 1], b2n[:, jj:jj + 1],
                        ALU.mult, ALU.add),
                        waits=[("wtr" + sfx, rec.n("wtr" + sfx)), ("r2m", 4)],
                        incs=[("wev" + sfx, 1)])
            wvw = ("wev" + sfx, rec.n("wev" + sfx))

        Bv = B.rearrange("p (c rk) -> p c rk", c=P)
        s1rot = [self.pbank[0], self.pbank[1], self.pbank[2], self.pbank[6]]
        for ch2 in range(64):
            pbank = s1rot[ch2 % 4]
            for ci in (0, 1):
                ch = 2 * ch2 + ci
                for l in (0, 1):
                    w = []
                    if ci == 0 and l == 0:
                        w = [wvw, ("cc_f1", self.cc_tgt["f1"]) if mode == "t"
                             else ("ccall", self.ccst_target)]
                        if rec.n(s_s1e) >= 4:
                            w.append((s_s1e, rec.n(s_s1e) - 3))
                        if mode == "t" and ch2 == 0 and rec.n(s_ptw) > 0:
                            w.append((s_ptw, rec.n(s_ptw)))
                            w.append(("nyev" + sfx, rec.n("nyev" + sfx)))
                    rec.add("tensor", lambda t, ch=ch, ci=ci, l=l, pbank=pbank: t.matmul(
                        pbank[64 * l:64 * l + 64, ci * 256:ci * 256 + 130],
                        winv[64 * l:64 * l + nzrows, :, ch],
                        f1_sb[64 * l:64 * l + nzrows, :],
                        start=True, stop=True, tile_position=(64 * l, 64 * l)),
                        waits=w, incs=[(s_s1, 1)] if (ci == 1 and l == 1) else ())
            def s1ev(e, ch2=ch2, pbank=pbank):
                pin = pbank.rearrange("p (ci rk) -> p ci rk", ci=2)[:, :, 0:130]
                ov = Bv[:, 2 * ch2:2 * ch2 + 2, 0:130]
                if hasattr(e, "tensor_copy"):
                    return e.tensor_copy(ov, pin)
                return e.copy(ov, pin)
            rec.add("vector" if ch2 % 2 == 0 else "scalar", s1ev,
                    waits=[(s_s1, rec.n(s_s1))], incs=[(s_s1e, 1)])
        s1e_total = rec.n(s_s1e)

        pn = self.pbank[2]
        rec.add("tensor", lambda t: t.matmul(
            pn[0:1, 0:128], self.misc[:, 1:2], Bv[:, :, 0], start=True, stop=True),
            waits=[(s_s1e, s1e_total), ("nyev" + sfx, rec.n("nyev" + sfx)), ("cst", 32)],
            incs=[("nymm" + sfx, 1)])
        if mode == "t":
            rec.add("vector", lambda v, g=g: v.tensor_scalar(
                self.tnyq[0:1, g * P:(g + 1) * P], pn[0:1, 0:128], 0.5, None, ALU.mult),
                waits=[("nymm" + sfx, rec.n("nymm" + sfx))], incs=[("nyev" + sfx, 1)])
        else:
            ypn, ynn, rowb = bufs["ypos"], bufs["yneg"], bufs["nyrow"]
            def nymath(v, g=g):
                v.tensor_mul(bufs["nyrowf"][0:1, :], pn[0:1, 0:128],
                             self.tnyq[0:1, g * P:(g + 1) * P])
                return v.tensor_scalar(rowb[0:1, :], bufs["nyrowf"][0:1, :],
                                       1.0 / NFFT, None, ALU.mult)
            rec.add("vector", nymath, waits=[("nymm" + sfx, rec.n("nymm" + sfx))],
                    incs=[("nyrow" + sfx, 1)])
            pnc = self.pbf[:, 0:128]
            rec.add("tensor", lambda t: t.transpose(
                pnc[:, 0:1], rowb[0:1, :], self.ident[0:1, 0:1]),
                waits=[("nyrow" + sfx, rec.n("nyrow" + sfx)),
                       ("nyev" + sfx, rec.n("nyev" + sfx))],
                incs=[("nytr" + sfx, 1)])
            def nycols(v):
                v.tensor_copy(ypn[:, 0:1], pnc[:, 0:1])
                return v.tensor_scalar(ynn[:, 0:1], pnc[:, 0:1], -1.0, None, ALU.mult)
            rec.add("vector", nycols, waits=[("nytr" + sfx, rec.n("nytr" + sfx))],
                    incs=[("nyev" + sfx, 1)])

        gv = g_sb.rearrange("p (s3 k q) -> p s3 k q", s3=3, k=P)
        if mode == "x":
            thre, thim = bufs["thre"], bufs["thim"]
            thld_base2 = [rec.n("thld0"), rec.n("thld1")]
            ptw_base = rec.n(s_ptw)
            for ck in (0, 1):
                for half, buf in ((0, thre), (1, thim)):
                    rec.add("sync", lambda sy, g=g, ck=ck, half=half, buf=buf: sy.dma_start(
                        out=buf[:, (ck % 2) * 1024:(ck % 2) * 1024 + 1024],
                        in_=self.s_that[:, g * 2 * SEQ + half * SEQ + ck * 1024:
                                        g * 2 * SEQ + half * SEQ + (ck + 1) * 1024]),
                        waits=[("thout", 16 * 2 * NGRP)] if (ck == 0 and half == 0) else [],
                        incs=[("thld%d" % (ck % 2), 16)])
        pre_rot = [self.pbank[3], self.pbank[5], self.pbank[0]]
        pim_rot = [self.pbank[4], self.pbank[6], self.pbank[1]]
        for jb in range(16):
            pre = pre_rot[jb % 3]
            pim = pim_rot[jb % 3]
            if mode == "x" and jb % 2 == 0 and jb // 2 + 2 < 8:
                ck = jb // 2 + 2
                for half, buf in ((0, bufs["thre"]), (1, bufs["thim"])):
                    rec.add("sync", lambda sy, g=g, ck=ck, half=half, buf=buf: sy.dma_start(
                        out=buf[:, (ck % 2) * 1024:(ck % 2) * 1024 + 1024],
                        in_=self.s_that[:, g * 2 * SEQ + half * SEQ + ck * 1024:
                                        g * 2 * SEQ + half * SEQ + (ck + 1) * 1024]),
                        waits=[(s_ptw, ptw_base + 2 * (ck - 1))] if half == 0 else [],
                        incs=[("thld%d" % (ck % 2), 16)])
            for jq in range(4):
                j = jb * 4 + jq
                m = 64 - j
                w0 = [(s_s1e, s1e_total)]
                if mode == "t" and jb == 0 and jq == 0:
                    w0.append(("cc_g", self.cc_tgt["g"]))
                rot = "xev" + sfx if mode == "x" else s_ptw
                if rec.n(rot) >= 3:
                    w0.append((rot, rec.n(rot) - 2))
                mm = [
                    (0, 0, j, 0, j, False), (0, 2, j, 1, j, True),
                    (1, 0, j, 1, j, False), (1, 1, j, 0, j, True),
                    (0, 0, 64 + j, 0, m, False), (0, 1, 64 + j, 1, m, True),
                    (1, 0, 64 + j, 1, m, False), (1, 2, 64 + j, 0, m, True),
                ]
                for (dst, gs, k2, br, bc, sp) in mm:
                    hi = k2 >= 64
                    pd = pim if dst else pre
                    rec.add("tensor", lambda t, pd=pd, gs=gs, k2=k2, br=br, bc=bc, jq=jq, hi=hi, sp=sp: t.matmul(
                        pd[64 * hi:64 * hi + 64, jq * 128:(jq + 1) * 128],
                        gv[:, gs, k2, :], Bv[:, :, br * 65 + bc],
                        start=(gs == 0), stop=sp, tile_position=(0, 64 * hi)),
                        waits=w0 if (dst == 0 and gs == 0 and not hi) else [],
                        incs=[(s_s2, 1)] if (dst == 1 and hi) and sp else ())
            if mode == "t":
                thre, thim = bufs["thre"], bufs["thim"]
                hf = (jb % 2) * 512
                def tev(e, pre=pre, pim=pim, hf=hf, jb=jb):
                    if jb == 0 and "th0" in self.dbg:
                        pass
                    if hasattr(e, "tensor_copy"):
                        e.tensor_copy(thre[:, hf:hf + 512], pre[:, :])
                        return e.tensor_copy(thim[:, hf:hf + 512], pim[:, :])
                    e.copy(thre[:, hf:hf + 512], pre[:, :])
                    return e.copy(thim[:, hf:hf + 512], pim[:, :])
                tvw = [(s_s2, rec.n(s_s2)), ("thout", rec.n("thout")),
                       ("dbgout", rec.n("dbgout"))]
                rec.add("vector" if jb % 2 == 0 else "scalar", tev, waits=tvw,
                        incs=[(s_ptw, 1)])
                if jb == 0:
                    rec.add("vector", lambda v: v.tensor_scalar(
                        thre[0:1, 0:128], thre[0:1, 0:128], 0.5, None, ALU.mult),
                        waits=[(s_ptw, rec.n(s_ptw))], incs=[("tbin0", 1)])
                if jb % 2 == 1:
                    ow = [(s_ptw, rec.n(s_ptw))]
                    if jb == 1:
                        ow.append(("tbin0", rec.n("tbin0")))
                    rec.add("sync", lambda sy, g=g, jb=jb: sy.dma_start(
                        out=self.s_that[:, g * 2 * SEQ + (jb - 1) * 512:
                                        g * 2 * SEQ + (jb + 1) * 512],
                        in_=thre[:, :]), waits=ow, incs=[("thout", 16)])
                    rec.add("sync", lambda sy, g=g, jb=jb: sy.dma_start(
                        out=self.s_that[:, g * 2 * SEQ + SEQ + (jb - 1) * 512:
                                        g * 2 * SEQ + SEQ + (jb + 1) * 512],
                        in_=thim[:, :]), incs=[("thout", 16)])
            else:
                tmp1 = bufs["ptmp2"] if jb % 4 == 3 else bufs["ptmp1"]
                xo = (jb % 2) * 512
                xre_sb = bufs["xre"][:, xo:xo + 512]
                xim_sb = bufs["xim"][:, xo:xo + 512]
                co = ((jb // 2) % 2) * 1024 + (jb % 2) * 512
                def xev(s, pre=pre, pim=pim, xre_sb=xre_sb, xim_sb=xim_sb):
                    s.copy(xre_sb, pre[:, :])
                    return s.copy(xim_sb, pim[:, :])
                xw = [(s_s2, rec.n(s_s2))]
                if rec.n(s_ptw) >= 2:
                    xw.append((s_ptw, rec.n(s_ptw) - 1))
                rec.add("scalar", xev, waits=xw, incs=[("xev" + sfx, 1)])
                ck = jb // 2
                def ptw(v, jb=jb, co=co, xre_sb=xre_sb, xim_sb=xim_sb, tmp1=tmp1):
                    tr = bufs["thre"][:, co:co + 512]
                    ti = bufs["thim"][:, co:co + 512]
                    yr = yh_re[:, jb * 512:(jb + 1) * 512]
                    yi = yh_im[:, jb * 512:(jb + 1) * 512]
                    v.tensor_mul(yr, xre_sb, tr)
                    v.tensor_mul(tmp1[:, :], xim_sb, ti)
                    v.tensor_sub(yr, yr, tmp1[:, :])
                    v.tensor_mul(yi, xre_sb, ti)
                    v.tensor_mul(tmp1[:, :], xim_sb, tr)
                    return v.tensor_add(yi, yi, tmp1[:, :])
                rec.add("gpsimd" if jb % 4 == 3 else "vector", ptw,
                        waits=[("xev" + sfx, rec.n("xev" + sfx)),
                               ("thld%d" % (ck % 2), 32 * (ck // 2 + 1) + thld_base2[ck % 2])],
                        incs=[(s_ptw, 1)])

        if mode == "t":
            return

        Cc, x2sl, x1sl = bufs["Cc"], bufs["x2sl"], bufs["x1sl"]
        ypn, ynn = bufs["ypos"], bufs["yneg"]
        ptw_total = rec.n(s_ptw)
        yre_v = yh_re.rearrange("p (k c) -> p k c", k=64)
        yim_v = yh_im.rearrange("p (k c) -> p k c", k=64)
        ccv = Cc.rearrange("p (c r n) -> p c r n", c=64, r=2)
        hv = hh_sb.rearrange("p (n s2 q) -> p n s2 q", n=P, s2=2)
        x1v2 = x1sl.rearrange("p (n2 n1) -> p n1 n2", n2=64)
        x2v = x2sl.rearrange("p (n1 n2) -> p n1 n2", n1=P)
        i1_rot = [self.pbank[0], self.pbank[1], self.pbank[3]]
        for hh in (0, 1):
            for cp in range(32):
                pbank = i1_rot[cp % 3]
                for ci in (0, 1):
                    ch = hh * 64 + 2 * cp + ci
                    for hi in (0, 1):
                        w = []
                        if ci == 0 and hi == 0:
                            w = [(s_ptw, ptw_total), ("i2" + sfx, rec.n("i2" + sfx))]
                            if rec.n("i1e" + sfx) >= 3:
                                w.append(("i1e" + sfx, rec.n("i1e" + sfx) - 2))
                        rec.add("tensor", lambda t, ch=ch, ci=ci, hi=hi, pbank=pbank: t.matmul(
                            pbank[64 * hi:64 * hi + 64, ci * 256:(ci + 1) * 256],
                            yre_v[64 * hi:64 * hi + 64, :, ch],
                            winv_sb[64 * hi:64 * hi + 64, 0:256],
                            start=True, stop=False, tile_position=(64 * hi, 64 * hi)),
                            waits=w)
                        rec.add("tensor", lambda t, ch=ch, ci=ci, hi=hi, pbank=pbank: t.matmul(
                            pbank[64 * hi:64 * hi + 64, ci * 256:(ci + 1) * 256],
                            yim_v[64 * hi:64 * hi + 64, :, ch],
                            winv_sb[64 * hi:64 * hi + 64, 256:512],
                            start=False, stop=True, tile_position=(64 * hi, 64 * hi)),
                            incs=[("i1" + sfx, 1)] if (ci == 1 and hi == 1) else ())
                def i1ev(e, cp=cp, pbank=pbank):
                    pin = pbank.rearrange("p (ci rn) -> p ci rn", ci=2)
                    ov = ccv[:, 2 * cp:2 * cp + 2, :, :].rearrange("p c r n -> p c (r n)")
                    if hasattr(e, "tensor_copy"):
                        return e.tensor_copy(ov, pin)
                    return e.copy(ov, pin)
                rec.add("vector" if cp % 4 == 0 else "scalar", i1ev,
                        waits=[("i1" + sfx, rec.n("i1" + sfx))],
                        incs=[("i1e" + sfx, 1)])
            i1e_half = rec.n("i1e" + sfx)
            i2_rot = [self.pbank[2], self.pbank[4], self.pbank[5]]
            for nb in range(16):
                pb = i2_rot[nb % 3]
                for nq in range(8):
                    n1 = nb * 8 + nq
                    w = []
                    if nq == 0:
                        w = [("i1e" + sfx, i1e_half)]
                        if rec.n("x2e" + sfx) >= 3:
                            w.append(("x2e" + sfx, rec.n("x2e" + sfx) - 2))
                    rec.add("tensor", lambda t, n1=n1, nq=nq, hh=hh, pb=pb: t.matmul(
                        pb[64 * hh:64 * hh + 64, nq * 64:(nq + 1) * 64],
                        ccv[:, :, 0, n1], hv[:, n1, 0, :],
                        start=True, stop=False, tile_position=(0, 64 * hh)),
                        waits=w)
                    rec.add("tensor", lambda t, n1=n1, nq=nq, hh=hh, pb=pb: t.matmul(
                        pb[64 * hh:64 * hh + 64, nq * 64:(nq + 1) * 64],
                        ccv[:, :, 1, n1], hv[:, n1, 1, :],
                        start=False, stop=True, tile_position=(0, 64 * hh)),
                        incs=[("i2" + sfx, 1)] if nq == 7 else ())
                pbv = pb.rearrange("p (nq n2) -> p nq n2", nq=8)
                def x2ev(v, nb=nb, hh=hh, pbv=pbv):
                    r0 = 64 * hh
                    for par in (0, 1):
                        ny = ypn if par == 0 else ynn
                        v.scalar_tensor_tensor(
                            x2v[r0:r0 + 64, nb * 8 + par:nb * 8 + 8:2, :],
                            pbv[r0:r0 + 64, par:8:2, :], ny[r0:r0 + 64, 0:1],
                            x1v2[r0:r0 + 64, nb * 8 + par:nb * 8 + 8:2, :],
                            ALU.add, ALU.add)
                    return v.engine_nop()
                rec.add("vector", x2ev,
                        waits=[("i2" + sfx, rec.n("i2" + sfx)),
                               ("nyev" + sfx, rec.n("nyev" + sfx))],
                        incs=[("x2e" + sfx, 1)])
        rec.add("sync", lambda sy, g=g: sy.dma_start(
            out=self.s_x2[g * P:(g + 1) * P, :], in_=x2sl[:, :]),
            waits=[("x2e" + sfx, rec.n("x2e" + sfx))], incs=[("x2out", 16)])
        if "x2" in self.dbg:
            rec.add("gpsimd", lambda gg, g=g: gg.dma_start(
                out=self.dbg["x2"][g * P:(g + 1) * P, :], in_=x2sl[:, :]),
                waits=[("x2e" + sfx, rec.n("x2e" + sfx))], incs=[("dbgout", 16)])
        if "b0" in self.dbg and g == 0:
            self.dbg_dump("b0", [(B, 0, 2 * 65 * P)], [(s_s1e, s1e_total)])
        if "yh0" in self.dbg and g == 0:
            self.dbg_dump("yh0", [(yh_re, 0, 64 * P), (yh_im, 64 * P, 64 * P)],
                          [(s_ptw, ptw_total)])
        if "win0" in self.dbg and g == 0:
            self.dbg_dump("win0", [(win, 0, 64 * P)], [("wev" + sfx, rec.n("wev" + sfx))])
        if "cc0" in self.dbg and g == 0:
            self.dbg_dump("cc0", [(Cc, 0, 64 * 2 * P)], [("i1e" + sfx, rec.n("i1e" + sfx))])

    def dbg_dump(self, key, pieces, waits):
        for (buf, off, ln) in pieces:
            self.rec.add("gpsimd", lambda gg, buf=buf, off=off, ln=ln, key=key: gg.dma_start(
                out=self.dbg[key][:, off:off + ln], in_=buf[:, 0:ln]),
                waits=waits, incs=[("dbgout", 16)])

    def p0b(self):
        self.mark()
        wins = [self.alloc("t_win0", [P, 64 * P], BF16),
                self.alloc("t_win1", [P, 64 * P], BF16)]
        bufs = {"win": wins[0],
                "B": self.alloc("t_B", [P, 2 * 65 * P], BF16),
                "yhre": None, "yhim": None,
                "thre": self.alloc("t_thre", [P, 1024], BF16),
                "thim": self.alloc("t_thim", [P, 1024], BF16)}
        for g in range(NGRP):
            bufs["win"] = wins[g % 2]
            self.conv_group(g, "t", bufs)
        self.release()

    def p2(self):
        rec = self.rec
        self.mark()
        bufs = {
            "x1sl": self.alloc("x1sl", [P, SEQ], BF16),
            "win": self.alloc("p2win", [P, 64 * P], BF16),
            "B": self.alloc("p2B", [P, 2 * 65 * P], BF16),
            "yhre": self.alloc("yhre", [P, 64 * P], BF16),
            "yhim": self.alloc("yhim", [P, 64 * P], BF16),
            "thre": self.alloc("thre", [P, 2048], BF16),
            "thim": self.alloc("thim", [P, 2048], BF16),

            "r2n": self.alloc("r2n", [P, P], F32),
            "b2n": self.alloc("b2n", [P, P], F32),
            "r2m": self.alloc("r2m", [P, 64], F32),
            "b2m": self.alloc("b2m", [P, 64], F32),
            "ypos": self.alloc("ypos", [P, 1], F32),
            "yneg": self.alloc("yneg", [P, 1], F32),
            "xre": self.alloc("xre", [P, 1024], BF16),
            "xim": self.alloc("xim", [P, 1024], BF16),
        }
        bufs["ptmp1"] = bufs["win"][:, 4096:4608]
        bufs["ptmp2"] = bufs["win"][:, 4608:5120]
        bufs["nyrow"] = bufs["ptmp1"][0:1, 0:P]
        bufs["nyrowf"] = bufs["xre"].bitcast(F32)[0:1, 0:P]
        bufs["Cc"] = bufs["B"][:, 0:64 * 2 * P]
        bufs["x2sl"] = bufs["win"][:, 0:SEQ]
        l2w = [("l2out", rec.n("l2out"))] + self.barrier_waits()
        for l in (0, 1):
            rec.add("gpsimd", lambda gg, l=l: gg.dma_start(
                out=bufs["r2n"][64 * l:64 * l + 64, :],
                in_=self.s_ln2[0:1, :].rearrange("o (n2 n1) -> (o n2) n1", n2=64)),
                waits=l2w, incs=[("r2ld", 16)])
            rec.add("gpsimd", lambda gg, l=l: gg.dma_start(
                out=bufs["b2n"][64 * l:64 * l + 64, :],
                in_=self.s_ln2[1:2, :].rearrange("o (n2 n1) -> (o n2) n1", n2=64)),
                waits=l2w, incs=[("r2ld", 16)])
        for l in (0, 1):
            rec.add("vector", lambda v, l=l: v.tensor_copy(
                bufs["r2m"][64 * l:64 * l + 64, :],
                bufs["r2n"].rearrange("p (jj l) -> p jj l", jj=64)[
                    64 * l:64 * l + 64, :, l]),
                waits=[("r2ld", rec.n("r2ld"))] if l == 0 else [],
                incs=[("r2m", 1)])
            rec.add("vector", lambda v, l=l: v.tensor_copy(
                bufs["b2m"][64 * l:64 * l + 64, :],
                bufs["b2n"].rearrange("p (jj l) -> p jj l", jj=64)[
                    64 * l:64 * l + 64, :, l]),
                waits=[], incs=[("r2m", 1)])
        pb2w = self.barrier_waits()
        for g in range(NGRP):
            rec.add("sync", lambda sy, g=g: sy.dma_start(
                out=bufs["x1sl"][:, :], in_=self.s_x1[g * P:(g + 1) * P, :]),
                waits=[("x1out", rec.n("x1out")), ("r2ld", rec.n("r2ld")),
                       ("x2ex", rec.n("x2ex")), ("x2out", rec.n("x2out")),
                       ("thout", rec.n("thout")), ("dbgout", rec.n("dbgout"))]
                      + (pb2w if g == 0 else []),
                incs=[("x1ld", 16)])
            self.conv_group(g, "x", bufs)
        self.release()

    def ffn_phase(self, tag):
        rec = self.rec
        self.mark()
        wgT = self.alloc(tag + "wgT", [P, 2 * NCH * P], BF16)
        wuT = self.alloc(tag + "wuT", [P, 2 * NCH * P], BF16)
        wdT = self.alloc(tag + "wdT", [P, NHT * DIM], BF16)
        nxs = 2
        xsl = [self.alloc(tag + f"xsl{i}", [P, NCH * SB], BF16) for i in range(nxs)]
        lnsl = [self.alloc(tag + f"lnsl{i}", [P, NCH * SB], BF16) for i in (0, 1)]
        hid = self.alloc(tag + "hid", [P, NHT * SB], BF16)
        silu = self.alloc(tag + "silu", [P, 2 * SB], BF16)
        rbp = self.alloc(tag + "rbp", [P, 2 * SB], BF16)
        rowf = self.alloc(tag + "rowf", [1, 5 * SB], F32)
        rows_bf = self.alloc(tag + "rowsbf", [1, 2 * SB], BF16)
        res = self.alloc(tag + "res", [P, NCH * SB], BF16 if tag == "f1" else F32)
        T = lambda s: tag + s
        f1 = tag == "f1"
        bw = self.barrier_waits()
        wgv = self.s_wT["g"].rearrange("p (c ht q) -> p ht c q", c=NCH, ht=NHT)
        wuv = self.s_wT["u"].rearrange("p (c ht q) -> p ht c q", c=NCH, ht=NHT)
        rec.add("sync", lambda sy: sy.dma_start(out=wdT[:, :], in_=self.s_wT["d"][:, :]),
                waits=[("wThbmd", 16)] + bw, incs=[(T("wldd"), 16)])

        def wstream(i, ht):
            sl = ht % 2
            gw2 = list(bw) if (i == 0 and ht < 2) else []
            rec.add("sync", lambda sy, ht=ht, sl=sl: sy.dma_start(
                out=wgT[:, sl * NCH * P:(sl + 1) * NCH * P], in_=wgv[:, ht, :, :]),
                waits=[("wThbmg", 16), (T("gmm"), rec.n(T("gmm")))] + gw2,
                incs=[(T("wldg%d" % sl), 16)])
            rec.add("scalar", lambda s, ht=ht, sl=sl: s.dma_start(
                out=wuT[:, sl * NCH * P:(sl + 1) * NCH * P], in_=wuv[:, ht, :, :]),
                waits=[("wThbmu", 16), (T("umm"), rec.n(T("umm")))] + gw2,
                incs=[(T("wldu%d" % sl), 16)])


        pstat = self.pbank[0]
        pstat2 = self.pbank[1]
        pg = [self.pbank[2], self.pbank[3]]
        pu = [self.pbank[4], self.pbank[5]]
        pdn = [self.pbank[6], self.pbf.bitcast(F32)]
        xin = (self.xT if f1 else self.s_x2).rearrange("(c p) s -> p c s", p=P)

        def prologue(i):
            bi = i % 2
            xs = i % nxs
            lw = []
            if i == 0:
                lw += bw
            if i >= nxs:
                lw.append((T("x1ev"), 8 * (i - nxs + 1)))
            if not f1:
                lw.append(("x2out", 16 * NGRP if "p2" in self.phases else 0))
            rec.add("gpsimd", lambda gg, i=i, xs=xs: gg.dma_start(
                out=xsl[xs].rearrange("p (c s) -> p c s", c=NCH),
                in_=xin[:, :, i * SB:(i + 1) * SB]),
                waits=[w for w in lw if w[1] > 0], incs=[(T("xld%d" % xs), 16)])
            sqw = [(T("xld%d" % xs), 16 * (i // nxs + 1)), (T("std"), rec.n(T("std")))]
            if i >= 2:
                sqw.append((T("umm"), rec.n(T("umm")) - NHT))
                if f1:
                    sqw.append((T("l2mm"), rec.n(T("l2mm"))))
            rec.add("scalar", lambda s, bi=bi, xs=xs: s.activation(
                lnsl[bi][:, :], xsl[xs][:, :], AF.Square),
                waits=sqw, incs=[(T("sqd"), 1)])
            stw = [(T("xld%d" % xs), 16 * (i // nxs + 1)), (T("row"), rec.n(T("row"))),
                   ("cst", 32), (T("l2row"), rec.n(T("l2row")))]
            for c in range(NCH):
                rec.add("tensor", lambda t, xs=xs, c=c: t.matmul(
                    pstat[0:1, 0:SB], self.misc[:, 0:1], xsl[xs][:, c * SB:(c + 1) * SB],
                    start=(c == 0), stop=(c == NCH - 1)),
                    waits=stw if c == 0 else [])
            for c in range(NCH):
                rec.add("tensor", lambda t, bi=bi, c=c: t.matmul(
                    pstat2[0:1, 0:SB], self.misc[:, 0:1], lnsl[bi][:, c * SB:(c + 1) * SB],
                    start=(c == 0), stop=(c == NCH - 1)),
                    waits=[(T("sqd"), rec.n(T("sqd")))] if c == 0 else [],
                    incs=[(T("std"), 1)] if c == NCH - 1 else ())
            rec.add("scalar", lambda s: s.activation(
                rowf[0:1, 3 * SB:4 * SB], pstat[0:1, 0:SB], AF.Square),
                waits=[(T("std"), rec.n(T("std"))), (T("rowq"), rec.n(T("rowq")))],
                incs=[(T("rowsq"), 1)])
            def rowq(v):
                q = rowf[0:1, 3 * SB:4 * SB]
                v.scalar_tensor_tensor(q, q, -1.0 / DIM, pstat2[0:1, 0:SB],
                                       ALU.mult, ALU.add)
                return v.tensor_scalar(q, q, 1.0 / DIM, EPS, ALU.mult, ALU.add)
            rec.add("vector", rowq,
                    waits=[(T("rowsq"), rec.n(T("rowsq"))), (T("std"), rec.n(T("std"))),
                           (T("rowr"), rec.n(T("rowr")))],
                    incs=[(T("rowq"), 1)])
            rec.add("scalar", lambda s: s.activation(
                rowf[0:1, 0:SB], rowf[0:1, 3 * SB:4 * SB], AF.Sqrt),
                waits=[(T("rowq"), rec.n(T("rowq"))), (T("row"), rec.n(T("row")))],
                incs=[(T("rowr"), 1)])
            def rowb(v):
                v.reciprocal(rowf[0:1, 0:SB], rowf[0:1, 0:SB])
                v.tensor_copy(rows_bf[0:1, 0:SB], rowf[0:1, 0:SB])
                return v.scalar_tensor_tensor(rows_bf[0:1, SB:2 * SB], pstat[0:1, 0:SB],
                                              -1.0 / DIM, rowf[0:1, 0:SB],
                                              ALU.mult, ALU.mult)
            rec.add("vector", rowb,
                    waits=[(T("rowr"), rec.n(T("rowr"))), (T("rbro"), rec.n(T("rbro")))],
                    incs=[(T("row"), 1)])
            rec.add("sync", lambda sy, i=i: sy.dma_start(
                out=self.s_rb[tag][i % 2:i % 2 + 1, :], in_=rows_bf[0:1, 0:2 * SB]),
                waits=[(T("row"), rec.n(T("row")))],
                incs=[(T("rbro"), 16)])
        def prologue_b(i):
            bi = i % 2
            xs = i % nxs
            rec.add("gpsimd", lambda gg, i=i: gg.dma_start(
                out=rbp[:, :], in_=self.s_rb[tag][i % 2:i % 2 + 1, :].broadcast_to([P, 2 * SB])),
                waits=[(T("rbro"), 16 * (i + 1)), (T("lnap"), rec.n(T("lnap")))],
                incs=[(T("rbld"), 16)])
            for c in range(NCH):
                def lnap(v, bi=bi, xs=xs, c=c):
                    v.tensor_mul(lnsl[bi][:, c * SB:(c + 1) * SB],
                                 xsl[xs][:, c * SB:(c + 1) * SB], rbp[:, 0:SB])
                    return v.tensor_add(lnsl[bi][:, c * SB:(c + 1) * SB],
                                        lnsl[bi][:, c * SB:(c + 1) * SB], rbp[:, SB:2 * SB])
                rec.add("vector", lnap,
                    waits=[(T("rbld"), rec.n(T("rbld"))), (T("std"), rec.n(T("std")))] if c == 0 else [],
                    incs=[(T("lnap"), 1)])

        def gate_up(i):
            bi = i % 2
            for ht in range(NHT):
                q = ht % 2
                sl = ht % 2
                nld = 16 * (i * NHT + ht + 1 + 1) if (i * NHT + ht + 2 <= NBLK * NHT) else 16 * NBLK * NHT
                gw = [(T("lnap"), rec.n(T("lnap"))),
                      (T("wldg%d" % sl), 16 * ((i * NHT + ht) // 2 + 1))]
                if rec.n(T("silu")) >= 2:
                    gw.append((T("silu"), rec.n(T("silu")) - 1))
                for c in range(NCH):
                    rec.add("tensor", lambda t, c=c, sl=sl, q=q: t.matmul(
                        pg[q][:, 0:SB], wgT[:, sl * NCH * P + c * P:sl * NCH * P + (c + 1) * P],
                        lnsl[i % 2][:, c * SB:(c + 1) * SB], start=(c == 0),
                        stop=(c == NCH - 1)),
                        waits=gw if c == 0 else [],
                        incs=[(T("gmm"), 1)] if c == NCH - 1 else ())
                uw = [(T("wldu%d" % sl), 16 * ((i * NHT + ht) // 2 + 1))]
                if rec.n(T("hmul")) >= 2:
                    uw.append((T("hmul"), rec.n(T("hmul")) - 1))
                for c in range(NCH):
                    rec.add("tensor", lambda t, c=c, sl=sl, q=q: t.matmul(
                        pu[q][:, 0:SB], wuT[:, sl * NCH * P + c * P:sl * NCH * P + (c + 1) * P],
                        lnsl[i % 2][:, c * SB:(c + 1) * SB], start=(c == 0),
                        stop=(c == NCH - 1)),
                        waits=uw if c == 0 else [],
                        incs=[(T("umm"), 1)] if c == NCH - 1 else ())
                nxt = i * NHT + ht + 2
                if nxt < NBLK * NHT:
                    wstream(nxt // NHT, nxt % NHT)
                sw = [(T("gmm"), rec.n(T("gmm")))]
                if rec.n(T("hmul")) >= 1:
                    sw.append((T("hmul"), rec.n(T("hmul")) - 1))
                rec.add("scalar", lambda s, ht=ht, q=q: s.activation(
                    silu[:, q * SB:(q + 1) * SB], pg[q][:, 0:SB], AF.Silu, scale=SCALE),
                    waits=sw, incs=[(T("silu"), 1)])
                hw = [(T("silu"), rec.n(T("silu"))), (T("umm"), rec.n(T("umm")))]
                if i > 0 and ht == 0:
                    hw.append((T("dmm"), rec.n(T("dmm"))))
                rec.add("vector", lambda v, ht=ht, q=q: v.tensor_mul(
                    hid[:, ht * SB:(ht + 1) * SB], silu[:, q * SB:(q + 1) * SB], pu[q][:, 0:SB]),
                    waits=hw, incs=[(T("hmul"), 1)])


        def down_store(i):
            bi = i % 2
            xs = i % nxs
            hm = rec.n(T("hmul"))
            for dt in range(NCH):
                pd = pdn[dt % 2]
                dw = []
                if dt == 0:
                    dw = [(T("hmul"), hm), (T("wldd"), 16)]
                    if i > 0:
                        dw.append((("x1out" if f1 else "o_out"), 16 * i))
                    if f1:
                        dw.append((T("l2row"), rec.n(T("l2row"))))
                    if "x1" in self.dbg and f1:
                        dw.append(("dbgout", rec.n("dbgout")))
                if dt >= 2:
                    dw.append((T("sdev"), rec.n(T("sdev")) - 1))
                for hc in range(NHT):
                    rec.add("tensor", lambda t, dt=dt, hc=hc, pd=pd: t.matmul(
                        pd[:, 0:SB],
                        wdT[:, hc * DIM + dt * P:hc * DIM + (dt + 1) * P],
                        hid[:, hc * SB:(hc + 1) * SB], start=(hc == 0), stop=(hc == NHT - 1)),
                        waits=dw if hc == 0 else [],
                        incs=[(T("dmm"), 1)] if hc == NHT - 1 else ())
                rec.add("scalar", lambda s, dt=dt, pd=pd: s.activation(
                    res[:, dt * SB:(dt + 1) * SB], pd[:, 0:SB], AF.Copy,
                    scale=0.5 * SCALE * SCALE),
                    waits=[(T("dmm"), rec.n(T("dmm")))],
                    incs=[(T("sdev"), 1)])
                rec.add("vector", lambda v, xs=xs, dt=dt: v.tensor_add(
                    res[:, dt * SB:(dt + 1) * SB], res[:, dt * SB:(dt + 1) * SB],
                    xsl[xs][:, dt * SB:(dt + 1) * SB]),
                    waits=[(T("sdev"), rec.n(T("sdev")))],
                    incs=[(T("x1ev"), 1)])
            if f1:
                x1o = self.s_x1.rearrange("(c p) s -> p c s", p=P)
                rec.add("sync", lambda sy, i=i: sy.dma_start(
                    out=x1o[:, :, i * SB:(i + 1) * SB],
                    in_=res.rearrange("p (c s) -> p c s", c=NCH)),
                    waits=[(T("x1ev"), rec.n(T("x1ev"))), ("x1out", 16 * i)],
                    incs=[("x1out", 16)])
                if "x1" in self.dbg:
                    d1 = self.dbg["x1"].rearrange("(c p) s -> p c s", p=P)
                    rec.add("gpsimd", lambda gg, i=i: gg.dma_start(
                        out=d1[:, :, i * SB:(i + 1) * SB],
                        in_=res.rearrange("p (c s) -> p c s", c=NCH)),
                        waits=[(T("x1ev"), rec.n(T("x1ev")))], incs=[("dbgout", 16)])
            else:
                oo = self.out_h.rearrange("(c p) s -> p c s", p=P)
                rec.add("sync", lambda sy, i=i: sy.dma_start(
                    out=oo[:, :, i * SB:(i + 1) * SB],
                    in_=res.rearrange("p (c s) -> p c s", c=NCH)),
                    waits=[(T("x1ev"), rec.n(T("x1ev"))), ("o_out", 16 * i)],
                    incs=[("o_out", 16)])
            if f1:
                rec.add("scalar", lambda s, bi=bi: s.activation(
                    lnsl[bi][:, :], res[:, :], AF.Square),
                    waits=[(T("x1ev"), rec.n(T("x1ev"))), (T("l2mm"), rec.n(T("l2mm"))),
                           (T("umm"), rec.n(T("umm")))],
                    incs=[(T("l2sq"), 1)])
                l2w0 = [(T("x1ev"), rec.n(T("x1ev"))), (T("l2row"), rec.n(T("l2row"))),
                        (T("row"), rec.n(T("row")))]
                for c in range(NCH):
                    rec.add("tensor", lambda t, c=c: t.matmul(
                        pstat[0:1, 0:SB], self.misc[:, 0:1],
                        res[:, c * SB:(c + 1) * SB], start=(c == 0), stop=(c == NCH - 1)),
                        waits=l2w0 if c == 0 else [])
                for c in range(NCH):
                    rec.add("tensor", lambda t, bi=bi, c=c: t.matmul(
                        pstat2[0:1, 0:SB], self.misc[:, 0:1],
                        lnsl[bi][:, c * SB:(c + 1) * SB], start=(c == 0), stop=(c == NCH - 1)),
                        waits=[(T("l2sq"), rec.n(T("l2sq")))] if c == 0 else [],
                        incs=[(T("l2mm"), 1)] if c == NCH - 1 else ())
                rec.add("scalar", lambda s: s.activation(
                    rowf[0:1, 4 * SB:5 * SB], pstat[0:1, 0:SB], AF.Square),
                    waits=[(T("l2mm"), rec.n(T("l2mm"))), (T("l2q"), rec.n(T("l2q")))],
                    incs=[(T("l2sq2"), 1)])
                def l2q(v):
                    q = rowf[0:1, 4 * SB:5 * SB]
                    v.scalar_tensor_tensor(q, q, -1.0 / DIM, pstat2[0:1, 0:SB],
                                           ALU.mult, ALU.add)
                    return v.tensor_scalar(q, q, 1.0 / DIM, EPS, ALU.mult, ALU.add)
                rec.add("vector", l2q,
                        waits=[(T("l2sq2"), rec.n(T("l2sq2"))), (T("l2mm"), rec.n(T("l2mm"))),
                               (T("l2r"), rec.n(T("l2r")))],
                        incs=[(T("l2q"), 1)])
                rec.add("scalar", lambda s: s.activation(
                    rowf[0:1, SB:2 * SB], rowf[0:1, 4 * SB:5 * SB], AF.Sqrt),
                    waits=[(T("l2q"), rec.n(T("l2q"))), ("l2out", rec.n("l2out"))],
                    incs=[(T("l2r"), 1)])
                def l2b(v):
                    v.reciprocal(rowf[0:1, SB:2 * SB], rowf[0:1, SB:2 * SB])
                    return v.scalar_tensor_tensor(
                        rowf[0:1, 2 * SB:3 * SB], pstat[0:1, 0:SB],
                        -1.0 / DIM, rowf[0:1, SB:2 * SB], ALU.mult, ALU.mult)
                rec.add("vector", l2b,
                        waits=[(T("l2r"), rec.n(T("l2r")))],
                        incs=[(T("l2row"), 1)])
                rec.add("sync", lambda sy, i=i: sy.dma_start(
                    out=self.s_ln2[0:1, i * SB:(i + 1) * SB], in_=rowf[0:1, SB:2 * SB]),
                    waits=[(T("l2row"), rec.n(T("l2row")))],
                    incs=[("l2out", 16)])
                rec.add("sync", lambda sy, i=i: sy.dma_start(
                    out=self.s_ln2[1:2, i * SB:(i + 1) * SB], in_=rowf[0:1, 2 * SB:3 * SB]),
                    incs=[("l2out", 16)])
                if "ln2rows" in self.dbg:
                    rec.add("gpsimd", lambda gg, i=i: gg.dma_start(
                        out=self.dbg["ln2rows"][0:1, i * SB:(i + 1) * SB],
                        in_=rowf[0:1, SB:2 * SB]),
                        waits=[(T("l2row"), rec.n(T("l2row")))], incs=[("dbgout", 16)])
                    rec.add("gpsimd", lambda gg, i=i: gg.dma_start(
                        out=self.dbg["ln2rows"][1:2, i * SB:(i + 1) * SB],
                        in_=rowf[0:1, 2 * SB:3 * SB]),
                        incs=[("dbgout", 16)])

        wstream(0, 0)
        wstream(0, 1)
        prologue(0)
        prologue_b(0)
        for i in range(NBLK):
            gate_up(i)
            if i + 1 < NBLK:
                prologue(i + 1)
                prologue_b(i + 1)
            down_store(i)
        self.release()

    def finalize(self):
        import contextlib
        nc, rec = self.nc, self.rec
        wnames = set()
        for e in ENGS:
            for wl, _, _, _ in rec.ops[e]:
                for s, v in wl:
                    wnames.add(s)
        names = sorted(set(rec.cnt.keys()) | wnames)
        assert len(names) <= 200, len(names)
        with contextlib.ExitStack() as sx:
            names = [s for s in names if s not in rec.owner]
            sems = {s: sx.enter_context(nc.semaphore(s)) for s in names}
            chv = sx.enter_context(nc.semaphore("chain_v"))
            chs = sx.enter_context(nc.semaphore("chain_s"))
            chg = sx.enter_context(nc.semaphore("chain_g"))
            chain_sems = {"vector": chv, "scalar": chs, "gpsimd": chg}
            with nc.Block() as block:
                block.tensor(lambda e: rec.emit(e, "tensor", sems, chain_sems))
                block.vector(lambda e: rec.emit(ChainedEngine(e, chv), "vector", sems, chain_sems))
                block.scalar(lambda e: rec.emit(ChainedEngine(e, chs), "scalar", sems, chain_sems))
                block.gpsimd(lambda e: rec.emit(ChainedEngine(e, chg), "gpsimd", sems, chain_sems))
                block.sync(lambda e: rec.emit(e, "sync", sems, chain_sems))
        self.stk.close()
        return nc


def build_nc(**kw):
    return KB(**kw).build().finalize()


_NC_CACHE = {}


def kernel(L0_attn_x, L0_attn_t, L0_ffn_w_gate, L0_ffn_w_up, L0_ffn_w_down):
    x = np.asarray(L0_attn_x)
    if "full" not in _NC_CACHE:
        _NC_CACHE["full"] = build_nc()
    nc = _NC_CACHE["full"]
    C = _host_constants()
    base = {"t": np.ascontiguousarray(np.asarray(L0_attn_t, dtype=np.float32)),
            "wg": np.ascontiguousarray(np.asarray(L0_ffn_w_gate, dtype=np.int32)),
            "wu": np.ascontiguousarray(np.asarray(L0_ffn_w_up, dtype=np.int32)),
            "wd": np.ascontiguousarray(np.asarray(L0_ffn_w_down, dtype=np.int32))}
    base.update(C)
    in_maps = []
    for c in range(8):
        m = dict(base)
        m["xT"] = np.ascontiguousarray(x[c].T.astype(np.float32))
        in_maps.append(m)
    res = run_bass_kernel_spmd(nc, in_maps, core_ids=list(range(8)))
    outs = []
    for c in range(8):
        o = res.results[c]["out"]
        outs.append(o.reshape(DIM, P, 64).transpose(2, 1, 0).reshape(SEQ, DIM))
    return np.stack(outs, axis=0).astype(np.float32)
